# version 14
# baseline (speedup 1.0000x reference)
import numpy as np
import ml_dtypes
import concourse.bass as bass
import concourse.mybir as mybir
from concourse.bass_utils import run_bass_kernel_spmd

F32 = mybir.dt.float32
BF16 = mybir.dt.bfloat16
AF = mybir.ActivationFunctionType
ALU = mybir.AluOpType
AX = mybir.AxisListType

P = 128
S = 2048
D = 1024
H = 16
HG = 8
DK = 64
NT = S // P
NCH = D // P

ENGS = ("tensor", "vector", "scalar", "gpsimd", "sync")


def ap3(base_ap, off_elems, dims):
    return bass.AP(tensor=base_ap.tensor, offset=base_ap.offset + off_elems,
                   ap=[list(d) for d in dims])


def row(t):
    return t[:].ap[0][0]


class KB:
    def __init__(self, nc):
        self.nc = nc
        self.items = {e: [] for e in ENGS}
        self.ninc = {e: 0 for e in ENGS}
        self.dma_cnt = {}
        self.dma_sems = {}
        self.eng_sems = {}

    def dma_sem(self, name):
        if name not in self.dma_sems:
            self.dma_sems[name] = self.nc.alloc_semaphore("ds_" + name)
            self.dma_cnt[name] = 0
        return self.dma_sems[name]

    def step(self, e, fn, after=(), inc=True):
        flat = []
        for x in after:
            if isinstance(x, list):
                flat.extend(x)
            elif x is not None:
                flat.append(x)
        after = tuple(x for x in flat if x is not None)
        for d in after:
            if d[0] == "d":
                assert d[2] <= self.dma_cnt[d[1]], (d, self.dma_cnt.get(d[1]))
            else:
                assert d[0] in ENGS and d[1] <= self.ninc[d[0]], (d, self.ninc)
        self.items[e].append((after, fn, inc))
        if inc:
            self.ninc[e] += 1
            return (e, self.ninc[e])
        return None

    def dma(self, e, name, fn, after=()):
        sem = self.dma_sem(name)
        self.dma_cnt[name] += 1
        k = self.dma_cnt[name]

        def run(eng):
            fn(eng).then_inc(sem, 16)
            return None

        self.step(e, run, after=after, inc=False)
        return ("d", name, k)

    def barrier(self):
        toks = [(e, self.ninc[e]) for e in ENGS if self.ninc[e] > 0]
        toks += [("d", n, c) for n, c in self.dma_cnt.items() if c > 0]
        return toks

    def emit(self):
        nc = self.nc
        for e in ENGS:
            self.eng_sems[e] = nc.alloc_semaphore("es_" + e)
        waited = {e: set() for e in ENGS}
        for e in ENGS:
            for after, fn, inc in self.items[e]:
                for d in after:
                    if d[0] != "d":
                        waited[d[0]].add(d[1])
        remap = {e: {idx: r + 1 for r, idx in enumerate(sorted(waited[e]))}
                 for e in ENGS}
        with nc.Block() as block:
            for e in ENGS:
                def body(eng, e=e):
                    water = {}
                    oi = 0
                    for after, fn, inc in self.items[e]:
                        need = {}
                        for d in after:
                            if d[0] == "d":
                                sem, val, key = self.dma_sems[d[1]], 16 * d[2], ("d", d[1])
                            else:
                                sem, val, key = self.eng_sems[d[0]], remap[d[0]][d[1]], d[0]
                            if val > need.get(key, (None, 0))[1]:
                                need[key] = (sem, val)
                        for key, (sem, val) in need.items():
                            if water.get(key, 0) < val:
                                eng.wait_ge(sem, val)
                                water[key] = val
                        last = fn(eng)
                        if inc:
                            oi += 1
                            assert last is not None
                            if oi in waited[e]:
                                last.then_inc(self.eng_sems[e], 1)
                getattr(block, e)(body)


def host_tables():
    k64 = np.arange(DK // 2, dtype=np.float64)
    dn32 = (10000.0 ** (2.0 * k64 / DK)).astype(np.float32)
    rcp = (np.float64(1.0) / dn32.astype(np.float64)).astype(np.float32)
    pos32 = np.arange(S, dtype=np.float32)
    ang = pos32[:, None] * rcp[None, :]
    import jax
    cpu = jax.devices("cpu")[0]
    with jax.default_device(cpu):
        import jax.numpy as jnp
        aj = jnp.asarray(ang)
        cos = np.asarray(jnp.cos(aj))
        sin = np.asarray(jnp.sin(aj))
    chat = np.repeat(cos, 2, axis=1)
    shat = np.empty((S, DK), np.float32)
    shat[:, 0::2] = -sin
    shat[:, 1::2] = sin
    eighth = np.float32(0.125)
    return chat, shat, chat * eighth, shat * eighth


def tile_major(a):
    w = a.shape[1]
    return np.ascontiguousarray(a.reshape(NT, P, w).transpose(1, 0, 2).reshape(P, NT * w))


def wt_layout(w):
    wt = np.ascontiguousarray(np.asarray(w, dtype=np.float32).T)
    return np.ascontiguousarray(wt.reshape(NCH, P, D).transpose(1, 0, 2).reshape(P, NCH * D))


def wt_half(w, g):
    full = wt_layout(w).reshape(P, NCH, D)
    return np.ascontiguousarray(full[:, :, 512 * g:512 * (g + 1)].reshape(P, NCH * 512))


def build_nc(dbg=None, dbg_tile=None):
    nc = bass.Bass("TRN2", target_bir_lowering=False)
    kb = KB(nc)

    x_in = nc.dram_tensor("x", [S, D], F32, kind="ExternalInput").ap()
    out_h = nc.dram_tensor("out", [S, D], F32, kind="ExternalOutput").ap()
    x1_scr = nc.dram_tensor("x1scr", [S, D], F32, kind="Internal").ap()
    if dbg:
        dbg_o = {nm: nc.dram_tensor("dbg_" + nm, shp, dt, kind="ExternalOutput").ap()
                 for nm, shp, dt in (
                     ("qT", [P, 4 * S], F32), ("kT", [P, 4 * S], F32),
                     ("v", [P, NT * HG * 80], F32),
                     ("ctxT", [P, NCH * S], F32), ("g", [80, S], F32))}

    wh_h = {}
    for nm in ("wq", "wk", "wv"):
        for g in range(2):
            wh_h[(nm, g)] = nc.dram_tensor(f"{nm}{g}", [P, NCH * 512], F32,
                                           kind="ExternalInput").ap()
    wf_h = {nm: nc.dram_tensor(nm, [P, NCH * D], F32, kind="ExternalInput").ap()
            for nm in ("wo", "wr", "wk2", "wv2")}
    b_h = {nm: nc.dram_tensor(nm, [D], F32, kind="ExternalInput").ap()
           for nm in ("bq", "bk", "bv", "bo")}
    b_h["mur"] = nc.dram_tensor("mur", [P, NCH], F32, kind="ExternalInput").ap()
    b_h["muk"] = nc.dram_tensor("muk", [P, NCH], F32, kind="ExternalInput").ap()
    tab_h = {nm: nc.dram_tensor(nm, [P, NT * DK], F32, kind="ExternalInput").ap()
             for nm in ("ck", "sk")}
    eyeb_h = nc.dram_tensor("eyeb", [P, P], BF16, kind="ExternalInput").ap()
    eyef_h = nc.dram_tensor("eyef", [P, P], F32, kind="ExternalInput").ap()
    mneg_h = nc.dram_tensor("mneg", [P, P], F32, kind="ExternalInput").ap()
    m01_h = nc.dram_tensor("m01", [P, P], BF16, kind="ExternalInput").ap()
    ehalf_h = nc.dram_tensor("ehalf", [2, P], BF16, kind="ExternalInput").ap()

    A = nc.alloc_sbuf_tensor

    eyeb = A("eyeb_sb", [P, P], BF16)
    eyef = A("eyef_sb", [P, P], F32)
    mneg = A("mneg_sb", [P, P], F32)
    m01 = A("m01_sb", [P, P], BF16)
    e0T = A("e0T", [P, P], BF16)
    ehalf = A("ehalf_sb", [2, P], BF16)
    brow = {nm: A(nm + "_row", [P, 512], BF16) for nm in ("bq", "bk", "bv")}
    mur_sb = A("mur_sb", [P, NCH], F32)
    muk_sb = A("muk_sb", [P, NCH], F32)
    ctxT = A("ctxT", [P, NCH * S], BF16)
    x_tile = [A(f"xt{i}", [P, D], F32) for i in range(2)]
    h_tile = [A(f"ht{i}", [P, D], BF16) for i in range(2)]
    slab = [A(f"slab{i}", [P, D], BF16) for i in range(2)]
    t1_sb = A("t1_sb", [P, D], F32)
    t2_sb = A("t2_sb", [P, D], F32)
    rope_bf = A("rope_bf", [P, 512], BF16)
    trash_bf = A("trash_bf", [P, D], BF16)
    stat = A("stat", [P, 24], F32)
    eps_col = A("eps_col", [P, 1], F32)
    er_rows = A("er_rows", [80, S], F32)
    g_rows = A("g_rows", [P, S], F32)
    r_all = A("r_all", [P, NT * H], F32)

    bd_guards = []

    def SG(name, shape, dtype):
        g = nc.sbuf_tensor(name, shape, dtype)
        t = g.__enter__()
        bd_guards.append(g)
        return t

    wq_sl = SG("wq_sl", [P, NCH * 512], BF16)
    wk_sl = SG("wk_sl", [P, NCH * 512], BF16)
    wv_sl = SG("wv_sl", [P, NCH * 512], BF16)
    qT = SG("qT", [P, 8 * S], BF16)
    kT = SG("kT", [P, 8 * S], BF16)
    v_sb = SG("v_sb", [P, NT * HG * 65 + 63], BF16)
    attn_sb = [SG(f"attn{i}", [P, S], BF16) for i in range(2)]
    neg_k0 = SG("neg_k0", [64, 8], F32)
    octxs = [SG("octx", [64, S], BF16), SG("octx2", [64, S], BF16)]
    pscr = SG("pscr", [P, HG], F32)
    rmx = SG("rmx", [P, 4 * HG], F32)
    dscr = SG("dscr", [P, 512], F32)
    rdg = SG("rdg", [P, HG], F32)
    tab_guards = []
    tab = {}
    for nm in ("ck", "sk"):
        g = nc.sbuf_tensor(nm + "_sb", [P, NT * DK], BF16)
        tab[nm] = g.__enter__()
        tab_guards.append(g)
    assert nc.sbuf_bytes_remaining >= 0

    ps_a = nc.alloc_psum_tensor("ps_a", [P, 1024], F32)
    ps_b = nc.alloc_psum_tensor("ps_b", [P, 1024], F32)

    for nm in ("ck", "sk"):
        kb.dma("gpsimd", "ldc2", lambda e, d=tab[nm], s=tab_h[nm]: e.dma_start(out=d[:], in_=s))
    for d, s in ((eyeb, eyeb_h), (eyef, eyef_h), (mneg, mneg_h), (m01, m01_h),
                 (ehalf, ehalf_h)):
        kb.dma("sync", "ldc", lambda e, d=d, s=s: e.dma_start(out=d[:], in_=s))
    for dst, s0 in ((mur_sb, b_h["mur"]), (muk_sb, b_h["muk"])):
        kb.dma("sync", "ldc", lambda e, d=dst, s=s0: e.dma_start(out=d[:], in_=s))

    t_z0 = kb.step("vector", lambda e: (
        e.memset(e0T[:], 0.0), e.memset(eps_col[:], 1e-5),
        e.memset(brow["bq"][:], 0.0), e.memset(brow["bk"][:], 0.0),
        e.memset(brow["bv"][:], 0.0),
        e.memset(qT[64:128, :], 0.0), e.memset(kT[64:128, :], 0.0),
        e.memset(v_sb[:, NT * HG * 65:NT * HG * 65 + 63], 0.0))[-1])
    t_ones = kb.step("vector", lambda e: e.memset(e0T[0:1, :], 1.0), after=[t_z0])

    ldc_all = ("d", "ldc", kb.dma_cnt["ldc"])
    ldc2_all = ("d", "ldc2", kb.dma_cnt["ldc2"])

    stat_free = [None, None, None]
    ln_ctr = [0]

    def ln_stats(src_ap, deps):
        sl_ = ln_ctr[0] % 3
        ln_ctr[0] += 1
        o = 8 * sl_
        deps = list(deps) + [stat_free[sl_]]
        t_sq = kb.step("scalar", lambda e, o=o: e.activation(
            out=trash_bf[:], in_=src_ap, func=AF.Square, accum_out=stat[:, o + 1:o + 2]),
            after=deps)
        t_sx = kb.step("vector", lambda e, o=o: e.reduce_sum(
            stat[:, o:o + 1], src_ap, axis=AX.X), after=deps)
        t_a = kb.step("vector", lambda e, o=o: e.tensor_scalar_mul(
            stat[:, o + 2:o + 3], stat[:, o:o + 1], -1.0 / D), after=[t_sx])
        t_b = kb.step("vector", lambda e, o=o: e.tensor_mul(
            stat[:, o + 3:o + 4], stat[:, o:o + 1], stat[:, o:o + 1]), after=[t_sx])
        t_c = kb.step("vector", lambda e, o=o: e.scalar_tensor_tensor(
            stat[:, o + 4:o + 5], stat[:, o + 3:o + 4], -1.0 / D, stat[:, o + 1:o + 2],
            ALU.mult, ALU.add),
            after=[t_b, t_sq])
        t_d = kb.step("scalar", lambda e, o=o: e.activation(
            out=stat[:, o + 5:o + 6], in_=stat[:, o + 4:o + 5], func=AF.Sqrt, scale=1.0 / D,
            bias=eps_col[:, 0:1]),
            after=[t_c, t_ones])
        t_e = kb.step("vector", lambda e, o=o: e.reciprocal(
            stat[:, o + 6:o + 7], stat[:, o + 5:o + 6]), after=[t_d])
        t_f = kb.step("vector", lambda e, o=o: e.tensor_mul(
            stat[:, o + 7:o + 8], stat[:, o + 2:o + 3], stat[:, o + 6:o + 7]),
            after=[t_e, t_a])
        return (t_f, sl_)

    def ln_apply(dst_ap, src_ap, t_f, extra):
        tok_f, sl_ = t_f
        o = 8 * sl_
        tok = kb.step("scalar", lambda e, o=o: e.activation(
            out=dst_ap, in_=src_ap, func=AF.Identity, scale=stat[:, o + 6:o + 7],
            bias=stat[:, o + 7:o + 8]),
            after=[tok_f] + list(extra))
        stat_free[sl_] = tok
        return tok

    psA_free = None
    psB_free = None
    g_row_toks = []
    ctx_write_toks = {}
    attn_free = [[], []]
    hpos = [0]
    prev_group_bar = None
    prev_w_free = None
    x_free = [None, None]
    h_free = [None, None]
    slab_free = [None, None]

    t_r = None
    for grp in (1, 0):
        tp_g = nc.psum_tensor(f"ps_tp{grp}", [P, 1024], BF16)
        ps_tp = tp_g.__enter__()
        tp2_g = nc.psum_tensor(f"ps_tq{grp}", [P, 1024], BF16)
        ps_tp2 = tp2_g.__enter__()
        c_g = nc.psum_tensor(f"ps_cc{grp}", [P, 1024], F32)
        ps_c = c_g.__enter__()

        wtok = {}
        w_dep = prev_w_free if prev_w_free else []
        for slot, nm in ((wq_sl, "wq"), (wk_sl, "wk"), (wv_sl, "wv")):
            kb.dma("gpsimd", "ldw", lambda e, d=slot, s=wh_h[(nm, grp)]:
                   e.dma_start(out=d[:], in_=s),
                   after=w_dep)
        for nm in ("bq", "bk", "bv"):
            kb.dma("gpsimd", "ldw", lambda e, d=brow[nm], s=b_h[nm], grp=grp: e.dma_start(
                out=d[0:1, :], in_=s[None, 512 * grp:512 * (grp + 1)]),
                after=[t_z0] + w_dep)
        wall = ("d", "ldw", kb.dma_cnt["ldw"])
        t_wq8 = kb.step("vector", lambda e: e.tensor_scalar_mul(
            wq_sl[:], wq_sl[:], 0.125), after=[wall])
        t_bq8 = kb.step("vector", lambda e: e.tensor_scalar_mul(
            brow["bq"][0:1, :], brow["bq"][0:1, :], 0.125),
            after=[wall])
        for nm in ("wq", "wk", "wv"):
            wtok[nm] = wall
        wtok["wq"] = t_wq8
        t_init = kb.step("vector", lambda e: e.memset(
            ap3(v_sb[:], 64, [[row(v_sb), P], [65 * HG, NT], [65, HG]]), 1.0),
            after=prev_group_bar or [])

        gdeps = list(prev_group_bar or [])
        tp_bufs = [ps_tp, ps_tp2]
        tp_frees = [gdeps, gdeps]
        tp_ctr = [0]

        def tp_next():
            bi = tp_ctr[0] % 2
            tp_ctr[0] += 1
            return bi

        last_qe = last_ke = last_ve = None
        ktile0_evac = None
        rope_free = None
        qe_tok = [None] * NT
        ksh = [None] * NT
        t_nk = None
        c_banks = [ap3(ps_c[:], 0, [[row(ps_c), P], [1, 512]]),
                   ap3(ps_c[:], 512, [[row(ps_c), P], [1, 512]]),
                   ap3(ps_b[:], 512, [[row(ps_b), P], [1, 512]])]
        c_bank_free = [gdeps, gdeps, gdeps]
        cch = 0
        t_comb = None
        dscr_free2 = gdeps

        def emit_C(i):
            nonlocal cch, t_r, t_comb, dscr_free2
            span = P * i
            nf = span // 512
            wl = span % 512
            ncols = nf + (1 if wl else 0)
            red_toks = []
            kdep = [qe_tok[i], ksh[i], ksh[0]]
            for lh in range(HG):
                seg = S * lh
                for ci in range(ncols):
                    w = 512 if ci < nf else wl
                    bk = cch % 3
                    cch += 1
                    dst = c_banks[bk]
                    mm = kb.step("tensor", lambda e, seg=seg, i=i, ci=ci, w=w, dst=dst: e.matmul(
                        ap3(dst, 0, [[dst.ap[0][0], P], [1, w]]),
                        qT[0:P, seg + P * i:seg + P * i + P],
                        kT[0:P, seg + 512 * ci:seg + 512 * ci + w],
                        start=True, stop=True),
                        after=[c_bank_free[bk]] + kdep)
                    tr1 = kb.step("vector", lambda e, dst=dst, w=w, lh=lh, ci=ci: e.reduce_max(
                        rmx[:, 4 * lh + ci:4 * lh + ci + 1],
                        ap3(dst, 0, [[dst.ap[0][0], P], [1, w]]), axis=AX.X),
                        after=[mm, t_comb])
                    c_bank_free[bk] = tr1
                    red_toks.append(tr1)
            for b2 in range(2):
                bk = cch % 3
                cch += 1
                dgb = c_banks[bk]
                dgl = None
                for q4 in range(4):
                    lh = 4 * b2 + q4
                    seg = S * lh
                    dgl = kb.step("tensor", lambda e, seg=seg, q4=q4, i=i, dgb=dgb: e.matmul(
                        ap3(dgb, P * q4, [[dgb.ap[0][0], P], [1, P]]),
                        qT[0:P, seg + P * i:seg + P * i + P],
                        kT[0:P, seg + P * i:seg + P * i + P],
                        start=(q4 == 0), stop=(q4 == 3)),
                        after=([c_bank_free[bk]] + kdep) if q4 == 0 else [])
                mb = ap3(mneg[:], 0, [[row(mneg), P], [0, 4], [1, P]])
                tmask = kb.step("vector", lambda e, mb=mb, dgb=dgb: e.scalar_tensor_tensor(
                    dscr[:], ap3(dgb, 0, [[dgb.ap[0][0], P], [1, 512]]), 1.0, mb,
                    ALU.mult, ALU.add),
                    after=[dgl, ldc_all, dscr_free2])
                c_bank_free[bk] = tmask
                tred = kb.step("vector", lambda e, b2=b2: e.reduce_max(
                    rdg[:, 4 * b2:4 * (b2 + 1)].rearrange("p (a o) -> p a o", o=1),
                    dscr[:].rearrange("p (a n) -> p a n", a=4), axis=AX.X),
                    after=[tmask, t_r])
                red_toks.append(tred)
                dscr_free2 = tred
            col = H * i + HG * grp
            if ncols > 0:
                t_comb = kb.step("vector", lambda e, ncols=ncols: e.reduce_max(
                    pscr[:, 0:HG].rearrange("p (a o) -> p a o", o=1),
                    ap3(rmx[:], 0, [[row(rmx), P], [4, HG], [1, ncols]]), axis=AX.X),
                    after=red_toks + [t_r])
                t_r = kb.step("vector", lambda e, col=col: e.tensor_max(
                    r_all[:, col:col + HG], pscr[:], rdg[:]), after=[t_comb])
            else:
                t_r = kb.step("vector", lambda e, col=col: e.tensor_copy(
                    r_all[:, col:col + HG], rdg[:]), after=red_toks)

        for i in range(NT):
            sl = i % 2
            tx = kb.dma("sync", f"ldx{sl}", lambda e, i=i, sl=sl: e.dma_start(
                out=x_tile[sl][:], in_=x_in[P * i:P * (i + 1), :]),
                after=[x_free[sl]])
            t_f = ln_stats(x_tile[sl][:], [tx])
            th = ln_apply(h_tile[sl][:], x_tile[sl][:], t_f, [h_free[sl]])
            x_free[sl] = th
            bi_h = tp_next()
            buf_h = tp_bufs[bi_h]
            tps = None
            for c in range(NCH):
                tps = kb.step("tensor", lambda e, sl=sl, c=c, buf_h=buf_h: e.matmul(
                    buf_h[:, P * c:P * (c + 1)], h_tile[sl][:, P * c:P * (c + 1)], eyeb[:],
                    is_transpose=True, start=(c == 0), stop=(c == NCH - 1)),
                    after=([th, ldc_all, tp_frees[bi_h]] + (gdeps if i == 0 else [])) if c == 0 else [])
            h_free[sl] = tps
            tsl = kb.step("scalar", lambda e, sl=sl, buf_h=buf_h: e.copy(
                slab[sl][:], buf_h[:]),
                after=[tps, slab_free[sl]])
            tp_frees[bi_h] = tsl

            def qkv_mm(dst, wslot, wname, bname, extra):
                last = None
                for c in range(NCH):
                    last = kb.step("tensor", lambda e, dst=dst, c=c, wslot=wslot, sl=sl: e.matmul(
                        dst, slab[sl][:, P * c:P * (c + 1)],
                        wslot[:, 512 * c:512 * (c + 1)],
                        start=(c == 0), stop=False),
                        after=([tsl, wtok[wname], ldc2_all, t_ones, t_bq8] + extra) if c == 0 else [])
                last = kb.step("tensor", lambda e, dst=dst, bname=bname: e.matmul(
                    dst, e0T[:], brow[bname][:],
                    start=False, stop=True))
                return last

            def rope(ps, cname, sname, dep, rfree):
                swap = ap3(ps, 1, [[ps.ap[0][0], P], [2, 256], [-1, 2]])
                cb = ap3(tab[cname][:], DK * i, [[row(tab[cname]), P], [0, HG], [1, DK]])
                sb_ = ap3(tab[sname][:], DK * i, [[row(tab[sname]), P], [0, HG], [1, DK]])
                view = ap3(ps, 0, [[ps.ap[0][0], P], [64, HG], [1, DK]])
                o1 = ap3(t1_sb[:], 0, [[row(t1_sb), P], [64, HG], [1, DK]])
                o2 = ap3(t2_sb[:], 0, [[row(t2_sb), P], [64, HG], [1, DK]])
                ta = kb.step("vector", lambda e, view=view, cb=cb, o1=o1: e.tensor_mul(o1, view, cb),
                             after=[dep, ldc_all])
                tb = kb.step("vector", lambda e, swap=swap, sb_=sb_, o2=o2: e.tensor_mul(o2, swap, sb_),
                             after=[dep])
                return kb.step("vector", lambda e: e.tensor_add(
                    rope_bf[:], t1_sb[:, 0:512], t2_sb[:, 0:512]),
                    after=[ta, tb, rfree])

            def tp_out(dep, bi):
                buf = tp_bufs[bi]
                tpl = None
                for c in range(8):
                    tpl = kb.step("tensor", lambda e, c=c, buf=buf: e.matmul(
                        buf[0:64, P * c:P * (c + 1)], rope_bf[:, 64 * c:64 * (c + 1)],
                        eyeb[:], is_transpose=True, start=True, stop=True),
                        after=[dep, tp_frees[bi]] if c == 0 else [])
                return tpl

            tq = qkv_mm(ps_a[:, 0:512], wq_sl, "wq", "bq", [psA_free])
            trq = rope(ps_a[:, 0:512], "ck", "sk", tq, rope_free)
            psA_free = trq
            bi_q = tp_next()
            tpl = tp_out(trq, bi_q)
            dst_q = ap3(qT[:], P * i, [[row(qT), 64], [S, 8], [1, P]])
            last_qe = kb.step("scalar", lambda e, dst_q=dst_q, buf_q=tp_bufs[bi_q]: e.copy(
                dst_q, buf_q[0:64, 0:1024].rearrange("p (c q) -> p c q", c=8)),
                after=[tpl, last_qe] + (gdeps if i == 0 else []))
            qe_tok[i] = last_qe
            tp_frees[bi_q] = last_qe

            tk = qkv_mm(ps_a[:, 512:1024], wk_sl, "wk", "bk", [psB_free])
            trk = rope(ps_a[:, 512:1024], "ck", "sk", tk, tpl)
            psB_free = trk
            bi_k = tp_next()
            tpl = tp_out(trk, bi_k)
            if i == 0:
                dst_k = ap3(kT[:], 0, [[row(kT), 64], [S, 8], [1, P]])
                last_ke = kb.step("scalar", lambda e, dst_k=dst_k, buf_k=tp_bufs[bi_k]: e.copy(
                    dst_k, buf_k[0:64, 0:1024].rearrange("p (c q) -> p c q", c=8)),
                    after=[tpl, last_ke] + gdeps)
                ktile0_evac = last_ke
                t_nk = kb.step("vector", lambda e: e.tensor_scalar_mul(
                    neg_k0[:], ap3(kT[:], 0, [[row(kT), 64], [S, 8]]), -1.0),
                    after=[ktile0_evac])
                tk0 = None
                for c in range(8):
                    tk0 = kb.step("vector", lambda e, c=c: e.tensor_scalar_add(
                        kT[0:64, S * c:S * c + P], kT[0:64, S * c:S * c + P],
                        neg_k0[:, c:c + 1]),
                        after=[t_nk])
                ksh[0] = tk0
            else:
                kev = None
                for c in range(8):
                    kev = kb.step("scalar", lambda e, c=c, i=i, buf_k=tp_bufs[bi_k]: e.activation(
                        out=ap3(kT[:], S * c + P * i, [[row(kT), 64], [1, P]]),
                        in_=buf_k[0:64, P * c:P * (c + 1)], func=AF.Identity,
                        bias=neg_k0[:, c:c + 1]),
                        after=[tpl, last_ke, t_nk] if c == 0 else [])
                last_ke = kev
                ksh[i] = kev
            tp_frees[bi_k] = last_ke
            rope_free = tpl

            tv = qkv_mm(ps_b[:, 0:512], wv_sl, "wv", "bv", [psB_free])
            slab_free[sl] = tv
            dst_v = ap3(v_sb[:], 65 * HG * i, [[row(v_sb), P], [65, HG], [1, DK]])
            last_ve = kb.step("vector", lambda e, dst_v=dst_v: e.tensor_copy(
                dst_v, ps_b[:, 0:512].rearrange("p (h d) -> p h d", h=HG)),
                after=[tv, t_init, last_ve] + (gdeps if i == 0 else []))
            psB_free = last_ve

            if i >= 1:
                emit_C(i - 1)

        emit_C(NT - 1)
        prev_w_free = [tq, tk, tv]
        t_qT, t_v = last_qe, last_ve
        t_k0 = [ksh[0], ksh[NT - 1]]

        if dbg == "B" and grp == 0:
            toks = []
            for nm, src in (("qT", qT), ("kT", kT), ("v", v_sb)):
                toks.append(kb.dma("gpsimd", "dbg", lambda e, nm=nm, src=src: e.dma_start(
                    out=dbg_o[nm], in_=src[:]), after=[t_qT, t_k0, t_v, t_init]))
            kb.step("sync", lambda e: e.wait_ge(kb.dma_sem("dbg"), 16 * len(toks)), inc=False)
            kb.emit()
            return nc

        cbar = kb.barrier()
        c_g.__exit__(None, None, None)
        tp2_g.__exit__(None, None, None)
        tp_g.__exit__(None, None, None)
        ctx_g = nc.psum_tensor(f"ps_ctx{grp}", [P, 2048], F32)
        ps_ctx = ctx_g.__enter__()

        ctx_free = cbar
        octx_free = [None, None]
        grow_free = {64: None, 96: None}
        sc_bank = [ap3(ps_a[:], 0, [[row(ps_a), P], [1, 512]]),
                   ap3(ps_a[:], 512, [[row(ps_a), P], [1, 512]]),
                   ap3(ps_b[:], 0, [[row(ps_b), P], [1, 512]]),
                   ap3(ps_b[:], 512, [[row(ps_b), P], [1, 512]])]
        sc_free = [cbar, cbar, cbar, cbar]
        ch_idx = 0
        for lh in reversed(range(HG)):
            hg = 8 * grp + lh
            seg = S * lh
            bank_last = {}
            first_dep = [t_v] + (ctx_free if isinstance(ctx_free, list) else [ctx_free])
            for j in range(NT):
                span = S - P * j
                buf = attn_sb[j % 2]
                ch_toks = []
                for c in range((span + 511) // 512):
                    w = min(512, span - 512 * c)
                    bk = ch_idx % 4
                    ch_idx += 1
                    dst = sc_bank[bk]
                    mml = kb.step("tensor", lambda e, seg=seg, j=j, c=c, w=w, dst=dst: e.matmul(
                        ap3(dst, 0, [[dst.ap[0][0], P], [1, w]]),
                        kT[0:P, seg + P * j:seg + P * j + P],
                        qT[0:P, seg + P * j + 512 * c:seg + P * j + 512 * c + w],
                        start=True, stop=True),
                        after=[sc_free[bk], t_qT, t_k0])
                    te = kb.step("scalar", lambda e, buf=buf, c=c, w=w, dst=dst: e.activation(
                        out=buf[:, 512 * c:512 * c + w],
                        in_=ap3(dst, 0, [[dst.ap[0][0], P], [1, w]]), func=AF.Exp),
                        after=[mml] + attn_free[j % 2])
                    sc_free[bk] = te
                    ch_toks.append(te)
                tm = kb.step("vector", lambda e, buf=buf: e.tensor_mul(
                    buf[:, 0:P], buf[:, 0:P], m01[:]), after=[ch_toks[0], ldc_all])
                last_c = None
                b_diag = (P * j) // 512
                for b in range(4):
                    cs = max(P * j, 512 * b)
                    ce = 512 * (b + 1)
                    if cs >= ce:
                        continue
                    need = [t for cc, t in enumerate(ch_toks)
                            if 512 * cc < (ce - P * j) and 512 * cc + 512 > (cs - P * j)]
                    dep = need + ([tm] if b == b_diag else []) \
                        + (first_dep if (j == 0 and b == 0) else [])
                    vst = ap3(v_sb[:], 65 * (HG * j + lh), [[row(v_sb), P], [1, P]])
                    last_c = kb.step("tensor", lambda e, vst=vst, j=j, cs=cs, ce=ce, b=b: e.matmul(
                        ps_ctx[0:P, cs:ce],
                        vst,
                        attn_sb[j % 2][:, cs - P * j:ce - P * j],
                        start=(j == 0), stop=(j == min(NT - 1, 4 * b + 3))),
                        after=dep)
                    bank_last[b] = last_c
                attn_free[j % 2] = [bank_last[b_diag], last_c]
            all_banks = list(bank_last.values())
            cblk = S * (hg // 2)
            if hg % 2 == 0:
                tev = kb.step("vector", lambda e, cblk=cblk: e.tensor_copy(
                    ctxT[0:64, cblk:cblk + S], ps_ctx[0:64, :]),
                    after=all_banks)
                ctx_write_toks.setdefault(hg // 2, []).append(tev)
                evac_done = tev
            else:
                ob = (hg // 2) % 2
                oc = octxs[ob]
                tcv = kb.step("vector", lambda e, oc=oc: e.tensor_copy(
                    oc[0:64, :], ps_ctx[0:64, :]),
                    after=all_banks + [octx_free[ob]])
                dsh = kb.dma("sync", f"shift{ob}", lambda e, cblk=cblk, oc=oc: e.dma_start(
                    out=ctxT[64:128, cblk:cblk + S],
                    in_=oc[0:64, :]), after=[tcv])
                octx_free[ob] = dsh
                ctx_write_toks.setdefault(hg // 2, []).append(dsh)
                evac_done = tcv
            sr = 64 if hpos[0] % 2 == 0 else 96
            hpos[0] += 1
            trow = kb.step("vector", lambda e, sr=sr: e.tensor_copy(
                g_rows[sr:sr + 1, :], ps_ctx[64:65, :]),
                after=all_banks + [grow_free[sr]])
            dmove = kb.dma("sync", f"gmv{sr}", lambda e, hg=hg, sr=sr: e.dma_start(
                out=g_rows[64 + hg:65 + hg, :], in_=g_rows[sr:sr + 1, :]),
                after=[trow] + ([grow_free[64]] if hg == 0 else []))
            grow_free[sr] = dmove
            g_row_toks.append(dmove)
            ctx_free = [evac_done, trow]

        gbar = kb.barrier()
        ctx_g.__exit__(None, None, None)
        prev_group_bar = gbar

    for g_ in reversed(tab_guards):
        g_.__exit__(None, None, None)
    tab_guards = []
    g_bf = kT

    rt_g = nc.psum_tensor("ps_rt", [P, 128], F32)
    ps_rt = rt_g.__enter__()
    rt_free = None
    t_rr = None
    for i in range(NT):
        tt = kb.step("tensor", lambda e, i=i: e.matmul(
            ps_rt[0:16, 0:P], r_all[:, H * i:H * (i + 1)], eyef[:],
            is_transpose=True, start=True, stop=True),
            after=[t_r, rt_free] + prev_group_bar if i == 0 else [rt_free])
        t_rr = kb.step("scalar", lambda e, i=i: e.copy(
            er_rows[0:16, P * i:P * (i + 1)], ps_rt[0:16, 0:P]), after=[tt])
        rt_free = t_rr
    dsh_r = kb.dma("sync", "shift", lambda e: e.dma_start(
        out=er_rows[64:80, :], in_=er_rows[0:16, :]), after=[t_rr])
    t_er = kb.step("scalar", lambda e: e.activation(
        out=er_rows[64:80, :], in_=er_rows[64:80, :], func=AF.Exp, scale=-1.0),
        after=[dsh_r])

    tg = kb.step("vector", lambda e: e.tensor_mul(
        g_rows[64:80, :], g_rows[64:80, :], er_rows[64:80, :]),
        after=g_row_toks + [t_er])
    tg = kb.step("vector", lambda e: e.tensor_scalar_add(
        g_rows[64:80, :], g_rows[64:80, :], 1e-8), after=[tg])
    tg = kb.step("vector", lambda e: e.reciprocal(
        g_rows[64:80, :], g_rows[64:80, :]), after=[tg])
    tg = kb.step("vector", lambda e: e.tensor_mul(
        g_rows[64:80, :], g_rows[64:80, :], er_rows[64:80, :]), after=[tg])

    for h in range(H):
        d_gbf = kb.dma("gpsimd", "gbf", lambda e, h=h: e.dma_start(
            out=ap3(g_bf[:], (h % 2) * row(g_bf) + S * (h // 2), [[row(g_bf), 1], [1, S]]),
            in_=g_rows[64 + h:65 + h, :]),
            after=[tg] if h == 0 else [])
    d_gbf = ("d", "gbf", H)

    gbp_g = nc.psum_tensor("ps_gbp", [P, 1024], F32)
    ps_gb = gbp_g.__enter__()
    norm_toks = []
    pg_free = [None, None]
    for c in range(NCH):
        for n0 in (0, 512, 1024, 1536):
            pb = (4 * c + n0 // 512) % 2
            mmg = kb.step("tensor", lambda e, c=c, n0=n0, pb=pb: e.matmul(
                ps_gb[:, 512 * pb:512 * (pb + 1)], ehalf[0:2, :],
                g_bf[0:2, S * c + n0:S * c + n0 + 512],
                start=True, stop=True),
                after=[d_gbf, ldc_all, pg_free[pb]] + (prev_group_bar if c == 0 and n0 == 0 else []))
            t_norm = kb.step("vector", lambda e, c=c, n0=n0, pb=pb: e.tensor_mul(
                ctxT[:, S * c + n0:S * c + n0 + 512],
                ctxT[:, S * c + n0:S * c + n0 + 512], ps_gb[:, 512 * pb:512 * (pb + 1)]),
                after=[mmg] + (ctx_write_toks[c] if n0 == 0 else []))
            pg_free[pb] = t_norm
            norm_toks.append(t_norm)

    if dbg == "D":
        toks = []
        for nm, src in (("ctxT", ctxT[:]), ("g", g_rows[64:80, :])):
            dst = {"ctxT": dbg_o["ctxT"], "g": dbg_o["g"][64:80, :]}[nm]
            toks.append(kb.dma("gpsimd", "dbg", lambda e, dst=dst, src=src: e.dma_start(
                out=dst, in_=src), after=norm_toks + [tg]))
        kb.step("sync", lambda e: e.wait_ge(kb.dma_sem("dbg"), 16 * len(toks)), inc=False)
        kb.emit()
        return nc

    dbar = kb.barrier()
    gbp_g.__exit__(None, None, None)
    rt_g.__exit__(None, None, None)
    for g in reversed(bd_guards):
        g.__exit__(None, None, None)

    efg_guards = []

    def EG(name, shape, dtype):
        g = nc.sbuf_tensor(name, shape, dtype)
        t = g.__enter__()
        efg_guards.append(g)
        return t

    SEG_W = S + 1
    h2T = EG("h2T", [P, NCH * SEG_W], BF16)
    xrT = EG("xrT", [P, NCH * 512], BF16)
    xkT = EG("xkT", [P, NCH * 512], BF16)
    r_tok = EG("r_tok", [P, 4 * D], BF16)
    kaT = EG("kaT", [P, NCH * 512], BF16)
    wsE = [EG(f"wsE{i}", [P, NCH * D], BF16) for i in range(3)]
    et1 = [t1_sb, t2_sb]
    et2 = [EG(f"et2_{i}", [P, D], F32) for i in range(2)]
    gf_sb = [t1_sb, t2_sb]
    gy_sb = et2
    gd_sb = EG("gd_sb", [P, 512], F32)
    bo_bc = EG("bo_bc", [P, D], BF16)
    assert nc.sbuf_bytes_remaining >= 0

    ps_tpE_g = nc.psum_tensor("ps_tpE", [P, 1024], BF16)
    ps_tpE = ps_tpE_g.__enter__()
    ps_c1_g = nc.psum_tensor("ps_c1", [P, 512], F32)
    ps_c1 = ps_c1_g.__enter__()
    ps_c2_g = nc.psum_tensor("ps_c2", [P, 512], F32)
    ps_c2 = ps_c2_g.__enter__()

    t_guard = kb.step("vector", lambda e: e.memset(
        ap3(h2T[:], 0, [[row(h2T), P], [SEG_W, NCH], [1, 1]]), 0.0), after=dbar)

    wtokE = {}
    for i_, nm in ((0, "wo"), (1, "wr"), (2, "wk2")):
        kb.dma("gpsimd", "ldw", lambda e, i_=i_, nm=nm: e.dma_start(
            out=wsE[i_][:], in_=wf_h[nm]), after=dbar)
    kb.dma("gpsimd", "ldw", lambda e: e.dma_start(
        out=bo_bc[:], in_=ap3(b_h["bo"], 0, [[0, 1], [0, P], [1, D]])), after=dbar)
    wallE = ("d", "ldw", kb.dma_cnt["ldw"])
    for nm in ("wo", "wr", "wk2"):
        wtokE[nm] = wallE

    x_free = [None, None]
    h_free = [None, None]
    tp_free = t_guard
    h2T_evac = None
    wv2_dep = None
    et1_free = [None, None]
    et2_free = [None, None]
    for i in range(NT):
        sl = i % 2
        dst = ps_a if sl == 0 else ps_b
        free = psA_free if sl == 0 else psB_free
        mml = None
        for c in range(NCH):
            for nseg in range(2):
                mml = kb.step("tensor", lambda e, c=c, nseg=nseg, i=i, dst=dst: e.matmul(
                    dst[:, 512 * nseg:512 * (nseg + 1)],
                    ctxT[:, S * c + P * i:S * c + P * i + P],
                    wsE[0][:, D * c + 512 * nseg:D * c + 512 * (nseg + 1)],
                    start=(c == 0), stop=(c == NCH - 1)),
                    after=([norm_toks[4 * c + (P * i) // 512]] if nseg == 0 else [])
                    + ([wtokE["wo"], free] if (c == 0 and nseg == 0) else []))
        if i == NT - 1:
            wv2_dep = mml
        tao = kb.step("vector", lambda e, dst=dst, sl=sl: e.tensor_add(
            et1[sl][:], dst[:], bo_bc[:]),
            after=[mml, wallE, et1_free[sl]])
        if sl == 0:
            psA_free = tao
        else:
            psB_free = tao
        t_f = ln_stats(et1[sl][:], [tao])
        ty1 = ln_apply(et2[sl][:], et1[sl][:], t_f, [et2_free[sl]])
        et1_free[sl] = ty1
        tx = kb.dma("sync", f"ldx{sl}", lambda e, i=i, sl=sl: e.dma_start(
            out=x_tile[sl][:], in_=x_in[P * i:P * (i + 1), :]),
            after=x_free[sl] if isinstance(x_free[sl], list) else [x_free[sl]])
        tx1 = kb.step("vector", lambda e, sl=sl: e.tensor_add(
            x_tile[sl][:], x_tile[sl][:], et2[sl][:]), after=[ty1, tx])
        et2_free[sl] = tx1
        dx1 = kb.dma("sync", f"stx1{sl}", lambda e, i=i, sl=sl: e.dma_start(
            out=x1_scr[P * i:P * (i + 1), :], in_=x_tile[sl][:]), after=[tx1])
        t_f2 = ln_stats(x_tile[sl][:], [tx1])
        th2 = ln_apply(h_tile[sl][:], x_tile[sl][:], t_f2, [h_free[sl]])
        x_free[sl] = [dx1, th2]
        tps = None
        for c in range(NCH):
            tps = kb.step("tensor", lambda e, sl=sl, c=c: e.matmul(
                ps_tpE[:, P * c:P * (c + 1)], h_tile[sl][:, P * c:P * (c + 1)], eyeb[:],
                is_transpose=True, start=(c == 0), stop=(c == NCH - 1)),
                after=[th2, tp_free] if c == 0 else [])
        h_free[sl] = tps
        dst_h = ap3(h2T[:], 1 + P * i, [[row(h2T), P], [SEG_W, NCH], [1, P]])
        h2T_evac = kb.step("scalar", lambda e, dst_h=dst_h: e.copy(
            dst_h, ps_tpE[:].rearrange("p (c q) -> p c q", c=NCH)),
            after=[tps, h2T_evac])
        tp_free = h2T_evac

    wtokE["wv2"] = kb.dma("gpsimd", "ldw", lambda e: e.dma_start(
        out=wsE[0][:], in_=wf_h["wv2"]), after=[wv2_dep])

    c1_free = c2_free = None
    x_free = [None, None]
    n_out = 0
    mm1_last = None
    mm2_last = None
    mm3_last = None
    tf_prev = [None] * 4
    mix_free = None
    gf_free = list(et1_free)
    gy_free = list(et2_free)
    for sc in range(4):
        t0 = 512 * sc
        mixr_toks = []
        mixk_toks = []
        tdl = None
        tk_ = None
        for c in range(NCH):
            cur = ap3(h2T[:], 1 + t0 + SEG_W * c, [[row(h2T), P], [1, 512]])
            prev = ap3(h2T[:], t0 + SEG_W * c, [[row(h2T), P], [1, 512]])
            tdl = kb.step("vector", lambda e, cur=cur, prev=prev: e.tensor_sub(
                gd_sb[:], cur, prev),
                after=[h2T_evac, t_guard, mix_free, tk_])
            tr = kb.step("vector", lambda e, c=c, prev=prev: e.scalar_tensor_tensor(
                xrT[:, 512 * c:512 * (c + 1)], gd_sb[:],
                mur_sb[:, c:c + 1], prev, ALU.mult, ALU.add),
                after=[tdl, ldc_all, mm1_last])
            tk_ = kb.step("vector", lambda e, c=c, prev=prev: e.scalar_tensor_tensor(
                xkT[:, 512 * c:512 * (c + 1)], gd_sb[:],
                muk_sb[:, c:c + 1], prev, ALU.mult, ALU.add), after=[tr, mm2_last])
            mixr_toks.append(tr)
            mixk_toks.append(tk_)
        mixr = mixr_toks[-1]
        mixk = mixk_toks[-1]
        mix_free = mixk
        sig_toks = []
        for u in range(4):
            sl = u % 2
            dst = ps_a if sl == 0 else ps_b
            free = psA_free if sl == 0 else psB_free
            mml = None
            for c in range(NCH):
                for nseg in range(2):
                    mml = kb.step("tensor", lambda e, c=c, nseg=nseg, u=u, dst=dst: e.matmul(
                        dst[:, 512 * nseg:512 * (nseg + 1)],
                        xrT[:, 512 * c + P * u:512 * c + P * u + P],
                        wsE[1][:, D * c + 512 * nseg:D * c + 512 * (nseg + 1)],
                        start=(c == 0), stop=(c == NCH - 1)),
                        after=([mixr_toks[c]] if nseg == 0 else [])
                        + ([wtokE["wr"], free] if (c == 0 and nseg == 0) else []))
            ts = kb.step("scalar", lambda e, u=u, dst=dst: e.activation(
                out=r_tok[:, D * u:D * (u + 1)], in_=dst[:], func=AF.Sigmoid),
                after=[mml, tf_prev[u]])
            sig_toks.append(ts)
            if u == 3:
                mm1_last = mml
            if sl == 0:
                psA_free = ts
            else:
                psB_free = ts
        relu_toks = []
        sq_toks = []
        for dc in range(NCH):
            dst = ps_c1 if dc % 2 == 0 else ps_c2
            free = c1_free if dc % 2 == 0 else c2_free
            mml = None
            for c in range(NCH):
                mml = kb.step("tensor", lambda e, c=c, dc=dc, dst=dst: e.matmul(
                    dst[:, 0:512],
                    wsE[2][:, D * c + P * dc:D * c + P * dc + P],
                    xkT[:, 512 * c:512 * (c + 1)],
                    start=(c == 0), stop=(c == NCH - 1)),
                    after=[mixk_toks[c]] + ([wtokE["wk2"], free] if c == 0 else []))
            tr_ = kb.step("scalar", lambda e, dc=dc, dst=dst: e.activation(
                out=kaT[:, 512 * dc:512 * (dc + 1)], in_=dst[:], func=AF.Relu),
                after=[mml, mm3_last if dc == 0 else None])
            relu_toks.append(tr_)
            tsq_c = kb.step("scalar", lambda e, dc=dc: e.activation(
                out=kaT[:, 512 * dc:512 * (dc + 1)],
                in_=kaT[:, 512 * dc:512 * (dc + 1)], func=AF.Square),
                after=[tr_])
            sq_toks.append(tsq_c)
            if dc % 2 == 0:
                c1_free = tr_
            else:
                c2_free = tr_
            if dc == NCH - 1:
                mm2_last = mml
        for u in range(4):
            gt = 4 * sc + u
            sl = u % 2
            dst = ps_a if sl == 0 else ps_b
            free = psA_free if sl == 0 else psB_free
            mml = None
            for c in range(NCH):
                for nseg in range(2):
                    mml = kb.step("tensor", lambda e, c=c, nseg=nseg, u=u, dst=dst: e.matmul(
                        dst[:, 512 * nseg:512 * (nseg + 1)],
                        kaT[:, 512 * c + P * u:512 * c + P * u + P],
                        wsE[0][:, D * c + 512 * nseg:D * c + 512 * (nseg + 1)],
                        start=(c == 0), stop=(c == NCH - 1)),
                        after=([sq_toks[c]] if nseg == 0 else [])
                        + ([wtokE["wv2"], free] if (c == 0 and nseg == 0) else []))
            tf = kb.step("vector", lambda e, u=u, dst=dst, sl=sl: e.tensor_mul(
                gf_sb[sl][:], dst[:], r_tok[:, D * u:D * (u + 1)]),
                after=[mml, sig_toks[u], gf_free[sl]])
            tf_prev[u] = tf
            if u == 3:
                mm3_last = mml
            if sl == 0:
                psA_free = tf
            else:
                psB_free = tf
            t_f3 = ln_stats(gf_sb[sl][:], [tf])
            ty2 = ln_apply(gy_sb[sl][:], gf_sb[sl][:], t_f3, [gy_free[sl]])
            gf_free[sl] = ty2
            tx = kb.dma("sync", f"ldx1{sl}", lambda e, gt=gt, sl=sl: e.dma_start(
                out=x_tile[sl][:], in_=x1_scr[P * gt:P * (gt + 1), :]),
                after=[x_free[sl], ("d", "stx10", NT // 2), ("d", "stx11", NT // 2)])
            tout = kb.step("vector", lambda e, sl=sl: e.tensor_add(
                x_tile[sl][:], x_tile[sl][:], gy_sb[sl][:]), after=[ty2, tx])
            gy_free[sl] = tout
            dout = kb.dma("sync", f"st{sl}", lambda e, gt=gt, sl=sl: e.dma_start(
                out=out_h[P * gt:P * (gt + 1), :], in_=x_tile[sl][:]), after=[tout])
            x_free[sl] = dout
            n_out += 1

    kb.step("sync", lambda e: (e.wait_ge(kb.dma_sem("st0"), 16 * (n_out // 2)),
                               e.wait_ge(kb.dma_sem("st1"), 16 * (n_out // 2)))[-1],
            inc=False)

    kb.emit()
    ps_c2_g.__exit__(None, None, None)
    ps_c1_g.__exit__(None, None, None)
    ps_tpE_g.__exit__(None, None, None)
    for g in reversed(efg_guards):
        g.__exit__(None, None, None)
    return nc


_NC_CACHE = None


def kernel(**inputs):
    global _NC_CACHE
    chat, shat, chat8, shat8 = host_tables()
    x = np.ascontiguousarray(np.asarray(inputs["L0_attn_x"], dtype=np.float32))
    base = {
        "wo": wt_layout(inputs["L0_attn_Wo"]),
        "wr": wt_layout(inputs["L0_ffn_W_r"]),
        "wk2": wt_layout(inputs["L0_ffn_W_k"]),
        "wv2": wt_layout(inputs["L0_ffn_W_v"]),
        "bq": np.ascontiguousarray(inputs["L0_attn_bq"], dtype=np.float32),
        "bk": np.ascontiguousarray(inputs["L0_attn_bk"], dtype=np.float32),
        "bv": np.ascontiguousarray(inputs["L0_attn_bv"], dtype=np.float32),
        "bo": np.ascontiguousarray(inputs["L0_attn_bo"], dtype=np.float32),
        "mur": np.ascontiguousarray(
            np.asarray(inputs["L0_ffn_mu_r"], np.float32).reshape(NCH, P).T),
        "muk": np.ascontiguousarray(
            np.asarray(inputs["L0_ffn_mu_k"], np.float32).reshape(NCH, P).T),
        "ck": tile_major(chat), "sk": tile_major(shat),
        "eyeb": np.eye(P, dtype=ml_dtypes.bfloat16),
        "eyef": np.eye(P, dtype=np.float32),
        "mneg": np.where(np.arange(P)[None, :] <= np.arange(P)[:, None],
                         0.0, -1e30).astype(np.float32),
        "m01": np.where(np.arange(P)[:, None] <= np.arange(P)[None, :],
                        1.0, 0.0).astype(ml_dtypes.bfloat16),
        "ehalf": (np.arange(P)[None, :] // 64 == np.arange(2)[:, None]
                  ).astype(ml_dtypes.bfloat16),
    }
    for g in range(2):
        base[f"wq{g}"] = wt_half(inputs["L0_attn_Wq"], g)
        base[f"wk{g}"] = wt_half(inputs["L0_attn_Wk"], g)
        base[f"wv{g}"] = wt_half(inputs["L0_attn_Wv"], g)
    if _NC_CACHE is None:
        _NC_CACHE = build_nc()
    nc = _NC_CACHE
    in_maps = []
    for b in range(8):
        m = dict(base)
        m["x"] = np.ascontiguousarray(x[b])
        in_maps.append(m)
    res = run_bass_kernel_spmd(nc, in_maps, core_ids=list(range(8)))
    return np.stack([res.results[b]["out"] for b in range(8)], axis=0)


# version 15 (final)
# speedup vs baseline: 1.0058x; 1.0058x over previous
import numpy as np
import ml_dtypes
import concourse.bass as bass
import concourse.mybir as mybir
from concourse.bass_utils import run_bass_kernel_spmd

F32 = mybir.dt.float32
BF16 = mybir.dt.bfloat16
AF = mybir.ActivationFunctionType
ALU = mybir.AluOpType
AX = mybir.AxisListType

P = 128
S = 2048
D = 1024
H = 16
HG = 8
DK = 64
NT = S // P
NCH = D // P

ENGS = ("tensor", "vector", "scalar", "gpsimd", "sync")


def ap3(base_ap, off_elems, dims):
    return bass.AP(tensor=base_ap.tensor, offset=base_ap.offset + off_elems,
                   ap=[list(d) for d in dims])


def row(t):
    return t[:].ap[0][0]


class KB:
    def __init__(self, nc):
        self.nc = nc
        self.items = {e: [] for e in ENGS}
        self.ninc = {e: 0 for e in ENGS}
        self.dma_cnt = {}
        self.dma_sems = {}
        self.eng_sems = {}

    def dma_sem(self, name):
        if name not in self.dma_sems:
            self.dma_sems[name] = self.nc.alloc_semaphore("ds_" + name)
            self.dma_cnt[name] = 0
        return self.dma_sems[name]

    def step(self, e, fn, after=(), inc=True):
        flat = []
        for x in after:
            if isinstance(x, list):
                flat.extend(x)
            elif x is not None:
                flat.append(x)
        after = tuple(x for x in flat if x is not None)
        for d in after:
            if d[0] == "d":
                assert d[2] <= self.dma_cnt[d[1]], (d, self.dma_cnt.get(d[1]))
            else:
                assert d[0] in ENGS and d[1] <= self.ninc[d[0]], (d, self.ninc)
        self.items[e].append((after, fn, inc))
        if inc:
            self.ninc[e] += 1
            return (e, self.ninc[e])
        return None

    def dma(self, e, name, fn, after=()):
        sem = self.dma_sem(name)
        self.dma_cnt[name] += 1
        k = self.dma_cnt[name]

        def run(eng):
            fn(eng).then_inc(sem, 16)
            return None

        self.step(e, run, after=after, inc=False)
        return ("d", name, k)

    def barrier(self):
        toks = [(e, self.ninc[e]) for e in ENGS if self.ninc[e] > 0]
        toks += [("d", n, c) for n, c in self.dma_cnt.items() if c > 0]
        return toks

    def emit(self):
        nc = self.nc
        for e in ENGS:
            self.eng_sems[e] = nc.alloc_semaphore("es_" + e)
        with nc.Block() as block:
            for e in ENGS:
                def body(eng, e=e):
                    water = {}
                    for after, fn, inc in self.items[e]:
                        need = {}
                        for d in after:
                            if d[0] == "d":
                                sem, val, key = self.dma_sems[d[1]], 16 * d[2], ("d", d[1])
                            else:
                                sem, val, key = self.eng_sems[d[0]], d[1], d[0]
                            if val > need.get(key, (None, 0))[1]:
                                need[key] = (sem, val)
                        for key, (sem, val) in need.items():
                            if water.get(key, 0) < val:
                                eng.wait_ge(sem, val)
                                water[key] = val
                        last = fn(eng)
                        if inc:
                            assert last is not None
                            last.then_inc(self.eng_sems[e], 1)
                getattr(block, e)(body)


def host_tables():
    k64 = np.arange(DK // 2, dtype=np.float64)
    dn32 = (10000.0 ** (2.0 * k64 / DK)).astype(np.float32)
    rcp = (np.float64(1.0) / dn32.astype(np.float64)).astype(np.float32)
    pos32 = np.arange(S, dtype=np.float32)
    ang = pos32[:, None] * rcp[None, :]
    import jax
    cpu = jax.devices("cpu")[0]
    with jax.default_device(cpu):
        import jax.numpy as jnp
        aj = jnp.asarray(ang)
        cos = np.asarray(jnp.cos(aj))
        sin = np.asarray(jnp.sin(aj))
    chat = np.repeat(cos, 2, axis=1)
    shat = np.empty((S, DK), np.float32)
    shat[:, 0::2] = -sin
    shat[:, 1::2] = sin
    eighth = np.float32(0.125)
    return chat, shat, chat * eighth, shat * eighth


def tile_major(a):
    w = a.shape[1]
    return np.ascontiguousarray(a.reshape(NT, P, w).transpose(1, 0, 2).reshape(P, NT * w))


def wt_layout(w):
    wt = np.ascontiguousarray(np.asarray(w, dtype=np.float32).T)
    return np.ascontiguousarray(wt.reshape(NCH, P, D).transpose(1, 0, 2).reshape(P, NCH * D))


def wt_half(w, g):
    full = wt_layout(w).reshape(P, NCH, D)
    return np.ascontiguousarray(full[:, :, 512 * g:512 * (g + 1)].reshape(P, NCH * 512))


def build_nc(dbg=None, dbg_tile=None):
    nc = bass.Bass("TRN2", target_bir_lowering=False)
    kb = KB(nc)

    x_in = nc.dram_tensor("x", [S, D], F32, kind="ExternalInput").ap()
    out_h = nc.dram_tensor("out", [S, D], F32, kind="ExternalOutput").ap()
    x1_scr = nc.dram_tensor("x1scr", [S, D], F32, kind="Internal").ap()
    if dbg:
        dbg_o = {nm: nc.dram_tensor("dbg_" + nm, shp, dt, kind="ExternalOutput").ap()
                 for nm, shp, dt in (
                     ("qT", [P, 4 * S], F32), ("kT", [P, 4 * S], F32),
                     ("v", [P, NT * HG * 80], F32),
                     ("ctxT", [P, NCH * S], F32), ("g", [80, S], F32))}

    wh_h = {}
    for nm in ("wq", "wk", "wv"):
        for g in range(2):
            wh_h[(nm, g)] = nc.dram_tensor(f"{nm}{g}", [P, NCH * 512], F32,
                                           kind="ExternalInput").ap()
    wf_h = {nm: nc.dram_tensor(nm, [P, NCH * D], F32, kind="ExternalInput").ap()
            for nm in ("wo", "wr", "wk2", "wv2")}
    b_h = {nm: nc.dram_tensor(nm, [D], F32, kind="ExternalInput").ap()
           for nm in ("bq", "bk", "bv", "bo")}
    b_h["mur"] = nc.dram_tensor("mur", [P, NCH], F32, kind="ExternalInput").ap()
    b_h["muk"] = nc.dram_tensor("muk", [P, NCH], F32, kind="ExternalInput").ap()
    tab_h = {nm: nc.dram_tensor(nm, [P, NT * DK], F32, kind="ExternalInput").ap()
             for nm in ("ck", "sk")}
    eyeb_h = nc.dram_tensor("eyeb", [P, P], BF16, kind="ExternalInput").ap()
    eyef_h = nc.dram_tensor("eyef", [P, P], F32, kind="ExternalInput").ap()
    mneg_h = nc.dram_tensor("mneg", [P, P], F32, kind="ExternalInput").ap()
    m01_h = nc.dram_tensor("m01", [P, P], BF16, kind="ExternalInput").ap()
    ehalf_h = nc.dram_tensor("ehalf", [2, P], BF16, kind="ExternalInput").ap()

    A = nc.alloc_sbuf_tensor

    eyeb = A("eyeb_sb", [P, P], BF16)
    eyef = A("eyef_sb", [P, P], F32)
    mneg = A("mneg_sb", [P, P], F32)
    m01 = A("m01_sb", [P, P], BF16)
    e0T = A("e0T", [P, P], BF16)
    ehalf = A("ehalf_sb", [2, P], BF16)
    brow = {nm: A(nm + "_row", [P, 512], BF16) for nm in ("bq", "bk", "bv")}
    mur_sb = A("mur_sb", [P, NCH], F32)
    muk_sb = A("muk_sb", [P, NCH], F32)
    ctxT = A("ctxT", [P, NCH * S], BF16)
    x_tile = [A(f"xt{i}", [P, D], F32) for i in range(2)]
    h_tile = [A(f"ht{i}", [P, D], BF16) for i in range(2)]
    slab = [A(f"slab{i}", [P, D], BF16) for i in range(2)]
    t1_sb = A("t1_sb", [P, D], F32)
    t2_sb = A("t2_sb", [P, D], F32)
    rope_bf = A("rope_bf", [P, 512], BF16)
    trash_bf = A("trash_bf", [P, D], BF16)
    stat = A("stat", [P, 24], F32)
    eps_col = A("eps_col", [P, 1], F32)
    er_rows = A("er_rows", [80, S], F32)
    g_rows = A("g_rows", [P, S], F32)
    r_all = A("r_all", [P, NT * H], F32)

    bd_guards = []

    def SG(name, shape, dtype):
        g = nc.sbuf_tensor(name, shape, dtype)
        t = g.__enter__()
        bd_guards.append(g)
        return t

    wq_sl = SG("wq_sl", [P, NCH * 512], BF16)
    wk_sl = SG("wk_sl", [P, NCH * 512], BF16)
    wv_sl = SG("wv_sl", [P, NCH * 512], BF16)
    qT = SG("qT", [P, 8 * S], BF16)
    kT = SG("kT", [P, 8 * S], BF16)
    v_sb = SG("v_sb", [P, NT * HG * 65 + 63], BF16)
    attn_sb = [SG(f"attn{i}", [P, S], BF16) for i in range(2)]
    neg_k0 = SG("neg_k0", [64, 8], F32)
    octxs = [SG("octx", [64, S], BF16), SG("octx2", [64, S], BF16)]
    pscr = SG("pscr", [P, HG], F32)
    rmx = SG("rmx", [P, 4 * HG], F32)
    dscr = SG("dscr", [P, 512], F32)
    rdg = SG("rdg", [P, HG], F32)
    tab_guards = []
    tab = {}
    for nm in ("ck", "sk"):
        g = nc.sbuf_tensor(nm + "_sb", [P, NT * DK], BF16)
        tab[nm] = g.__enter__()
        tab_guards.append(g)
    assert nc.sbuf_bytes_remaining >= 0

    ps_a = nc.alloc_psum_tensor("ps_a", [P, 1024], F32)
    ps_b = nc.alloc_psum_tensor("ps_b", [P, 1024], F32)

    for nm in ("ck", "sk"):
        kb.dma("gpsimd", "ldc2", lambda e, d=tab[nm], s=tab_h[nm]: e.dma_start(out=d[:], in_=s))
    for d, s in ((eyeb, eyeb_h), (eyef, eyef_h), (mneg, mneg_h), (m01, m01_h),
                 (ehalf, ehalf_h)):
        kb.dma("sync", "ldc", lambda e, d=d, s=s: e.dma_start(out=d[:], in_=s))
    for dst, s0 in ((mur_sb, b_h["mur"]), (muk_sb, b_h["muk"])):
        kb.dma("sync", "ldc", lambda e, d=dst, s=s0: e.dma_start(out=d[:], in_=s))

    t_z0 = kb.step("vector", lambda e: (
        e.memset(e0T[:], 0.0), e.memset(eps_col[:], 1e-5),
        e.memset(brow["bq"][:], 0.0), e.memset(brow["bk"][:], 0.0),
        e.memset(brow["bv"][:], 0.0),
        e.memset(qT[64:128, :], 0.0), e.memset(kT[64:128, :], 0.0),
        e.memset(v_sb[:, NT * HG * 65:NT * HG * 65 + 63], 0.0))[-1])
    t_ones = kb.step("vector", lambda e: e.memset(e0T[0:1, :], 1.0), after=[t_z0])

    ldc_all = ("d", "ldc", kb.dma_cnt["ldc"])
    ldc2_all = ("d", "ldc2", kb.dma_cnt["ldc2"])

    stat_free = [None, None, None]
    ln_ctr = [0]

    def ln_stats(src_ap, deps):
        sl_ = ln_ctr[0] % 3
        ln_ctr[0] += 1
        o = 8 * sl_
        deps = list(deps) + [stat_free[sl_]]
        t_sq = kb.step("scalar", lambda e, o=o: e.activation(
            out=trash_bf[:], in_=src_ap, func=AF.Square, accum_out=stat[:, o + 1:o + 2]),
            after=deps)
        t_sx = kb.step("vector", lambda e, o=o: e.reduce_sum(
            stat[:, o:o + 1], src_ap, axis=AX.X), after=deps)
        t_a = kb.step("vector", lambda e, o=o: e.tensor_scalar_mul(
            stat[:, o + 2:o + 3], stat[:, o:o + 1], -1.0 / D), after=[t_sx])
        t_b = kb.step("vector", lambda e, o=o: e.tensor_mul(
            stat[:, o + 3:o + 4], stat[:, o:o + 1], stat[:, o:o + 1]), after=[t_sx])
        t_c = kb.step("vector", lambda e, o=o: e.scalar_tensor_tensor(
            stat[:, o + 4:o + 5], stat[:, o + 3:o + 4], -1.0 / D, stat[:, o + 1:o + 2],
            ALU.mult, ALU.add),
            after=[t_b, t_sq])
        t_d = kb.step("scalar", lambda e, o=o: e.activation(
            out=stat[:, o + 5:o + 6], in_=stat[:, o + 4:o + 5], func=AF.Sqrt, scale=1.0 / D,
            bias=eps_col[:, 0:1]),
            after=[t_c, t_ones])
        t_e = kb.step("vector", lambda e, o=o: e.reciprocal(
            stat[:, o + 6:o + 7], stat[:, o + 5:o + 6]), after=[t_d])
        t_f = kb.step("vector", lambda e, o=o: e.tensor_mul(
            stat[:, o + 7:o + 8], stat[:, o + 2:o + 3], stat[:, o + 6:o + 7]),
            after=[t_e, t_a])
        return (t_f, sl_)

    def ln_apply(dst_ap, src_ap, t_f, extra):
        tok_f, sl_ = t_f
        o = 8 * sl_
        tok = kb.step("scalar", lambda e, o=o: e.activation(
            out=dst_ap, in_=src_ap, func=AF.Identity, scale=stat[:, o + 6:o + 7],
            bias=stat[:, o + 7:o + 8]),
            after=[tok_f] + list(extra))
        stat_free[sl_] = tok
        return tok

    psA_free = None
    psB_free = None
    g_row_toks = []
    ctx_write_toks = {}
    attn_free = [[], []]
    hpos = [0]
    prev_group_bar = None
    prev_w_free = None
    x_free = [None, None]
    h_free = [None, None]
    slab_free = [None, None]

    t_r = None
    for grp in (1, 0):
        tp_g = nc.psum_tensor(f"ps_tp{grp}", [P, 1024], BF16)
        ps_tp = tp_g.__enter__()
        tp2_g = nc.psum_tensor(f"ps_tq{grp}", [P, 1024], BF16)
        ps_tp2 = tp2_g.__enter__()
        c_g = nc.psum_tensor(f"ps_cc{grp}", [P, 1024], F32)
        ps_c = c_g.__enter__()

        w_dep = prev_w_free if prev_w_free else []
        wdt = {}
        for slot, nm in ((wq_sl, "wq"), (wk_sl, "wk"), (wv_sl, "wv")):
            wdt[nm] = kb.dma("gpsimd", f"ldw_{nm}{grp}",
                             lambda e, d=slot, s=wh_h[(nm, grp)]:
                             e.dma_start(out=d[:], in_=s),
                             after=w_dep)
        bdt = {}
        for nm in ("bq", "bk", "bv"):
            bdt[nm] = kb.dma("gpsimd", f"ldw_{nm}{grp}",
                             lambda e, d=brow[nm], s=b_h[nm], grp=grp: e.dma_start(
                                 out=d[0:1, :], in_=s[None, 512 * grp:512 * (grp + 1)]),
                             after=[t_z0] + w_dep)
        t_wq8 = kb.step("vector", lambda e: e.tensor_scalar_mul(
            wq_sl[:], wq_sl[:], 0.125), after=[wdt["wq"]])
        t_bq8 = kb.step("vector", lambda e: e.tensor_scalar_mul(
            brow["bq"][0:1, :], brow["bq"][0:1, :], 0.125),
            after=[bdt["bq"]])
        wtok = {"wq": t_wq8, "wk": wdt["wk"], "wv": wdt["wv"]}
        btok = {"bq": t_bq8, "bk": bdt["bk"], "bv": bdt["bv"]}
        t_init = kb.step("vector", lambda e: e.memset(
            ap3(v_sb[:], 64, [[row(v_sb), P], [65 * HG, NT], [65, HG]]), 1.0),
            after=prev_group_bar or [])

        gdeps = list(prev_group_bar or [])
        tp_bufs = [ps_tp, ps_tp2]
        tp_frees = [gdeps, gdeps]
        tp_ctr = [0]

        def tp_next():
            bi = tp_ctr[0] % 2
            tp_ctr[0] += 1
            return bi

        last_qe = last_ke = last_ve = None
        ktile0_evac = None
        rope_free = None
        qe_tok = [None] * NT
        ksh = [None] * NT
        t_nk = None
        c_banks = [ap3(ps_c[:], 0, [[row(ps_c), P], [1, 512]]),
                   ap3(ps_c[:], 512, [[row(ps_c), P], [1, 512]]),
                   ap3(ps_b[:], 512, [[row(ps_b), P], [1, 512]])]
        c_bank_free = [gdeps, gdeps, gdeps]
        cch = 0
        t_comb = None
        dscr_free2 = gdeps

        def emit_C(i):
            nonlocal cch, t_r, t_comb, dscr_free2
            span = P * i
            nf = span // 512
            wl = span % 512
            ncols = nf + (1 if wl else 0)
            red_toks = []
            kdep = [qe_tok[i], ksh[i], ksh[0]]
            for lh in range(HG):
                seg = S * lh
                for ci in range(ncols):
                    w = 512 if ci < nf else wl
                    bk = cch % 3
                    cch += 1
                    dst = c_banks[bk]
                    mm = kb.step("tensor", lambda e, seg=seg, i=i, ci=ci, w=w, dst=dst: e.matmul(
                        ap3(dst, 0, [[dst.ap[0][0], P], [1, w]]),
                        qT[0:P, seg + P * i:seg + P * i + P],
                        kT[0:P, seg + 512 * ci:seg + 512 * ci + w],
                        start=True, stop=True),
                        after=[c_bank_free[bk]] + kdep)
                    tr1 = kb.step("vector", lambda e, dst=dst, w=w, lh=lh, ci=ci: e.reduce_max(
                        rmx[:, 4 * lh + ci:4 * lh + ci + 1],
                        ap3(dst, 0, [[dst.ap[0][0], P], [1, w]]), axis=AX.X),
                        after=[mm, t_comb])
                    c_bank_free[bk] = tr1
                    red_toks.append(tr1)
            for b2 in range(2):
                bk = cch % 3
                cch += 1
                dgb = c_banks[bk]
                dgl = None
                for q4 in range(4):
                    lh = 4 * b2 + q4
                    seg = S * lh
                    dgl = kb.step("tensor", lambda e, seg=seg, q4=q4, i=i, dgb=dgb: e.matmul(
                        ap3(dgb, P * q4, [[dgb.ap[0][0], P], [1, P]]),
                        qT[0:P, seg + P * i:seg + P * i + P],
                        kT[0:P, seg + P * i:seg + P * i + P],
                        start=(q4 == 0), stop=(q4 == 3)),
                        after=([c_bank_free[bk]] + kdep) if q4 == 0 else [])
                mb = ap3(mneg[:], 0, [[row(mneg), P], [0, 4], [1, P]])
                tmask = kb.step("vector", lambda e, mb=mb, dgb=dgb: e.scalar_tensor_tensor(
                    dscr[:], ap3(dgb, 0, [[dgb.ap[0][0], P], [1, 512]]), 1.0, mb,
                    ALU.mult, ALU.add),
                    after=[dgl, ldc_all, dscr_free2])
                c_bank_free[bk] = tmask
                tred = kb.step("vector", lambda e, b2=b2: e.reduce_max(
                    rdg[:, 4 * b2:4 * (b2 + 1)].rearrange("p (a o) -> p a o", o=1),
                    dscr[:].rearrange("p (a n) -> p a n", a=4), axis=AX.X),
                    after=[tmask, t_r])
                red_toks.append(tred)
                dscr_free2 = tred
            col = H * i + HG * grp
            if ncols > 0:
                t_comb = kb.step("vector", lambda e, ncols=ncols: e.reduce_max(
                    pscr[:, 0:HG].rearrange("p (a o) -> p a o", o=1),
                    ap3(rmx[:], 0, [[row(rmx), P], [4, HG], [1, ncols]]), axis=AX.X),
                    after=red_toks + [t_r])
                t_r = kb.step("vector", lambda e, col=col: e.tensor_max(
                    r_all[:, col:col + HG], pscr[:], rdg[:]), after=[t_comb])
            else:
                t_r = kb.step("vector", lambda e, col=col: e.tensor_copy(
                    r_all[:, col:col + HG], rdg[:]), after=red_toks)

        for i in range(NT):
            sl = i % 2
            tx = kb.dma("sync", f"ldx{sl}", lambda e, i=i, sl=sl: e.dma_start(
                out=x_tile[sl][:], in_=x_in[P * i:P * (i + 1), :]),
                after=[x_free[sl]])
            t_f = ln_stats(x_tile[sl][:], [tx])
            th = ln_apply(h_tile[sl][:], x_tile[sl][:], t_f, [h_free[sl]])
            x_free[sl] = th
            bi_h = tp_next()
            buf_h = tp_bufs[bi_h]
            tps = None
            for c in range(NCH):
                tps = kb.step("tensor", lambda e, sl=sl, c=c, buf_h=buf_h: e.matmul(
                    buf_h[:, P * c:P * (c + 1)], h_tile[sl][:, P * c:P * (c + 1)], eyeb[:],
                    is_transpose=True, start=(c == 0), stop=(c == NCH - 1)),
                    after=([th, ldc_all, tp_frees[bi_h]] + (gdeps if i == 0 else [])) if c == 0 else [])
            h_free[sl] = tps
            tsl = kb.step("scalar", lambda e, sl=sl, buf_h=buf_h: e.copy(
                slab[sl][:], buf_h[:]),
                after=[tps, slab_free[sl]])
            tp_frees[bi_h] = tsl

            def qkv_mm(dst, wslot, wname, bname, extra):
                last = None
                for c in range(NCH):
                    last = kb.step("tensor", lambda e, dst=dst, c=c, wslot=wslot, sl=sl: e.matmul(
                        dst, slab[sl][:, P * c:P * (c + 1)],
                        wslot[:, 512 * c:512 * (c + 1)],
                        start=(c == 0), stop=False),
                        after=([tsl, wtok[wname], btok[bname], ldc2_all, t_ones]
                               + extra) if c == 0 else [])
                last = kb.step("tensor", lambda e, dst=dst, bname=bname: e.matmul(
                    dst, e0T[:], brow[bname][:],
                    start=False, stop=True))
                return last

            def rope(ps, cname, sname, dep, rfree):
                swap = ap3(ps, 1, [[ps.ap[0][0], P], [2, 256], [-1, 2]])
                cb = ap3(tab[cname][:], DK * i, [[row(tab[cname]), P], [0, HG], [1, DK]])
                sb_ = ap3(tab[sname][:], DK * i, [[row(tab[sname]), P], [0, HG], [1, DK]])
                view = ap3(ps, 0, [[ps.ap[0][0], P], [64, HG], [1, DK]])
                o1 = ap3(t1_sb[:], 0, [[row(t1_sb), P], [64, HG], [1, DK]])
                o2 = ap3(t2_sb[:], 0, [[row(t2_sb), P], [64, HG], [1, DK]])
                ta = kb.step("vector", lambda e, view=view, cb=cb, o1=o1: e.tensor_mul(o1, view, cb),
                             after=[dep, ldc_all])
                tb = kb.step("vector", lambda e, swap=swap, sb_=sb_, o2=o2: e.tensor_mul(o2, swap, sb_),
                             after=[dep])
                return kb.step("vector", lambda e: e.tensor_add(
                    rope_bf[:], t1_sb[:, 0:512], t2_sb[:, 0:512]),
                    after=[ta, tb, rfree])

            def tp_out(dep, bi):
                buf = tp_bufs[bi]
                tpl = None
                for c in range(8):
                    tpl = kb.step("tensor", lambda e, c=c, buf=buf: e.matmul(
                        buf[0:64, P * c:P * (c + 1)], rope_bf[:, 64 * c:64 * (c + 1)],
                        eyeb[:], is_transpose=True, start=True, stop=True),
                        after=[dep, tp_frees[bi]] if c == 0 else [])
                return tpl

            tq = qkv_mm(ps_a[:, 0:512], wq_sl, "wq", "bq", [psA_free])
            trq = rope(ps_a[:, 0:512], "ck", "sk", tq, rope_free)
            psA_free = trq
            bi_q = tp_next()
            tpl = tp_out(trq, bi_q)
            dst_q = ap3(qT[:], P * i, [[row(qT), 64], [S, 8], [1, P]])
            last_qe = kb.step("scalar", lambda e, dst_q=dst_q, buf_q=tp_bufs[bi_q]: e.copy(
                dst_q, buf_q[0:64, 0:1024].rearrange("p (c q) -> p c q", c=8)),
                after=[tpl, last_qe] + (gdeps if i == 0 else []))
            qe_tok[i] = last_qe
            tp_frees[bi_q] = last_qe

            tk = qkv_mm(ps_a[:, 512:1024], wk_sl, "wk", "bk", [psB_free])
            trk = rope(ps_a[:, 512:1024], "ck", "sk", tk, tpl)
            psB_free = trk
            bi_k = tp_next()
            tpl = tp_out(trk, bi_k)
            if i == 0:
                dst_k = ap3(kT[:], 0, [[row(kT), 64], [S, 8], [1, P]])
                last_ke = kb.step("scalar", lambda e, dst_k=dst_k, buf_k=tp_bufs[bi_k]: e.copy(
                    dst_k, buf_k[0:64, 0:1024].rearrange("p (c q) -> p c q", c=8)),
                    after=[tpl, last_ke] + gdeps)
                ktile0_evac = last_ke
                t_nk = kb.step("vector", lambda e: e.tensor_scalar_mul(
                    neg_k0[:], ap3(kT[:], 0, [[row(kT), 64], [S, 8]]), -1.0),
                    after=[ktile0_evac])
                tk0 = None
                for c in range(8):
                    tk0 = kb.step("vector", lambda e, c=c: e.tensor_scalar_add(
                        kT[0:64, S * c:S * c + P], kT[0:64, S * c:S * c + P],
                        neg_k0[:, c:c + 1]),
                        after=[t_nk])
                ksh[0] = tk0
            else:
                kev = None
                for c in range(8):
                    kev = kb.step("scalar", lambda e, c=c, i=i, buf_k=tp_bufs[bi_k]: e.activation(
                        out=ap3(kT[:], S * c + P * i, [[row(kT), 64], [1, P]]),
                        in_=buf_k[0:64, P * c:P * (c + 1)], func=AF.Identity,
                        bias=neg_k0[:, c:c + 1]),
                        after=[tpl, last_ke, t_nk] if c == 0 else [])
                last_ke = kev
                ksh[i] = kev
            tp_frees[bi_k] = last_ke
            rope_free = tpl

            tv = qkv_mm(ps_b[:, 0:512], wv_sl, "wv", "bv", [psB_free])
            slab_free[sl] = tv
            dst_v = ap3(v_sb[:], 65 * HG * i, [[row(v_sb), P], [65, HG], [1, DK]])
            last_ve = kb.step("vector", lambda e, dst_v=dst_v: e.tensor_copy(
                dst_v, ps_b[:, 0:512].rearrange("p (h d) -> p h d", h=HG)),
                after=[tv, t_init, last_ve] + (gdeps if i == 0 else []))
            psB_free = last_ve

            if i >= 1:
                emit_C(i - 1)

        emit_C(NT - 1)
        prev_w_free = [tq, tk, tv]
        t_qT, t_v = last_qe, last_ve
        t_k0 = [ksh[0], ksh[NT - 1]]

        if dbg == "B" and grp == 0:
            toks = []
            for nm, src in (("qT", qT), ("kT", kT), ("v", v_sb)):
                toks.append(kb.dma("gpsimd", "dbg", lambda e, nm=nm, src=src: e.dma_start(
                    out=dbg_o[nm], in_=src[:]), after=[t_qT, t_k0, t_v, t_init]))
            kb.step("sync", lambda e: e.wait_ge(kb.dma_sem("dbg"), 16 * len(toks)), inc=False)
            kb.emit()
            return nc

        cbar = kb.barrier()
        c_g.__exit__(None, None, None)
        tp2_g.__exit__(None, None, None)
        tp_g.__exit__(None, None, None)
        ctx_g = nc.psum_tensor(f"ps_ctx{grp}", [P, 2048], F32)
        ps_ctx = ctx_g.__enter__()

        ctx_free = cbar
        octx_free = [None, None]
        grow_free = {64: None, 96: None}
        sc_bank = [ap3(ps_a[:], 0, [[row(ps_a), P], [1, 512]]),
                   ap3(ps_a[:], 512, [[row(ps_a), P], [1, 512]]),
                   ap3(ps_b[:], 0, [[row(ps_b), P], [1, 512]]),
                   ap3(ps_b[:], 512, [[row(ps_b), P], [1, 512]])]
        sc_free = [cbar, cbar, cbar, cbar]
        ch_idx = 0
        for lh in reversed(range(HG)):
            hg = 8 * grp + lh
            seg = S * lh
            bank_last = {}
            first_dep = [t_v] + (ctx_free if isinstance(ctx_free, list) else [ctx_free])
            for j in range(NT):
                span = S - P * j
                buf = attn_sb[j % 2]
                ch_toks = []
                for c in range((span + 511) // 512):
                    w = min(512, span - 512 * c)
                    bk = ch_idx % 4
                    ch_idx += 1
                    dst = sc_bank[bk]
                    mml = kb.step("tensor", lambda e, seg=seg, j=j, c=c, w=w, dst=dst: e.matmul(
                        ap3(dst, 0, [[dst.ap[0][0], P], [1, w]]),
                        kT[0:P, seg + P * j:seg + P * j + P],
                        qT[0:P, seg + P * j + 512 * c:seg + P * j + 512 * c + w],
                        start=True, stop=True),
                        after=[sc_free[bk], t_qT, t_k0])
                    te = kb.step("scalar", lambda e, buf=buf, c=c, w=w, dst=dst: e.activation(
                        out=buf[:, 512 * c:512 * c + w],
                        in_=ap3(dst, 0, [[dst.ap[0][0], P], [1, w]]), func=AF.Exp),
                        after=[mml] + attn_free[j % 2])
                    sc_free[bk] = te
                    ch_toks.append(te)
                tm = kb.step("vector", lambda e, buf=buf: e.tensor_mul(
                    buf[:, 0:P], buf[:, 0:P], m01[:]), after=[ch_toks[0], ldc_all])
                last_c = None
                b_diag = (P * j) // 512
                for b in range(4):
                    cs = max(P * j, 512 * b)
                    ce = 512 * (b + 1)
                    if cs >= ce:
                        continue
                    need = [t for cc, t in enumerate(ch_toks)
                            if 512 * cc < (ce - P * j) and 512 * cc + 512 > (cs - P * j)]
                    dep = need + ([tm] if b == b_diag else []) \
                        + (first_dep if (j == 0 and b == 0) else [])
                    vst = ap3(v_sb[:], 65 * (HG * j + lh), [[row(v_sb), P], [1, P]])
                    last_c = kb.step("tensor", lambda e, vst=vst, j=j, cs=cs, ce=ce, b=b: e.matmul(
                        ps_ctx[0:P, cs:ce],
                        vst,
                        attn_sb[j % 2][:, cs - P * j:ce - P * j],
                        start=(j == 0), stop=(j == min(NT - 1, 4 * b + 3))),
                        after=dep)
                    bank_last[b] = last_c
                attn_free[j % 2] = [bank_last[b_diag], last_c]
            all_banks = list(bank_last.values())
            cblk = S * (hg // 2)
            if hg % 2 == 0:
                tev = kb.step("vector", lambda e, cblk=cblk: e.tensor_copy(
                    ctxT[0:64, cblk:cblk + S], ps_ctx[0:64, :]),
                    after=all_banks)
                ctx_write_toks.setdefault(hg // 2, []).append(tev)
                evac_done = tev
            else:
                ob = (hg // 2) % 2
                oc = octxs[ob]
                tcv = kb.step("vector", lambda e, oc=oc: e.tensor_copy(
                    oc[0:64, :], ps_ctx[0:64, :]),
                    after=all_banks + [octx_free[ob]])
                dsh = kb.dma("sync", f"shift{ob}", lambda e, cblk=cblk, oc=oc: e.dma_start(
                    out=ctxT[64:128, cblk:cblk + S],
                    in_=oc[0:64, :]), after=[tcv])
                octx_free[ob] = dsh
                ctx_write_toks.setdefault(hg // 2, []).append(dsh)
                evac_done = tcv
            sr = 64 if hpos[0] % 2 == 0 else 96
            hpos[0] += 1
            trow = kb.step("vector", lambda e, sr=sr: e.tensor_copy(
                g_rows[sr:sr + 1, :], ps_ctx[64:65, :]),
                after=all_banks + [grow_free[sr]])
            dmove = kb.dma("sync", f"gmv{sr}", lambda e, hg=hg, sr=sr: e.dma_start(
                out=g_rows[64 + hg:65 + hg, :], in_=g_rows[sr:sr + 1, :]),
                after=[trow] + ([grow_free[64]] if hg == 0 else []))
            grow_free[sr] = dmove
            g_row_toks.append(dmove)
            ctx_free = [evac_done, trow]

        gbar = kb.barrier()
        ctx_g.__exit__(None, None, None)
        prev_group_bar = gbar

    for g_ in reversed(tab_guards):
        g_.__exit__(None, None, None)
    tab_guards = []
    g_bf = kT

    rt_g = nc.psum_tensor("ps_rt", [P, 128], F32)
    ps_rt = rt_g.__enter__()
    rt_free = None
    t_rr = None
    for i in range(NT):
        tt = kb.step("tensor", lambda e, i=i: e.matmul(
            ps_rt[0:16, 0:P], r_all[:, H * i:H * (i + 1)], eyef[:],
            is_transpose=True, start=True, stop=True),
            after=[t_r, rt_free] + prev_group_bar if i == 0 else [rt_free])
        t_rr = kb.step("scalar", lambda e, i=i: e.copy(
            er_rows[0:16, P * i:P * (i + 1)], ps_rt[0:16, 0:P]), after=[tt])
        rt_free = t_rr
    dsh_r = kb.dma("sync", "shift", lambda e: e.dma_start(
        out=er_rows[64:80, :], in_=er_rows[0:16, :]), after=[t_rr])
    t_er = kb.step("scalar", lambda e: e.activation(
        out=er_rows[64:80, :], in_=er_rows[64:80, :], func=AF.Exp, scale=-1.0),
        after=[dsh_r])

    tg = kb.step("vector", lambda e: e.tensor_mul(
        g_rows[64:80, :], g_rows[64:80, :], er_rows[64:80, :]),
        after=g_row_toks + [t_er])
    tg = kb.step("vector", lambda e: e.tensor_scalar_add(
        g_rows[64:80, :], g_rows[64:80, :], 1e-8), after=[tg])
    tg = kb.step("vector", lambda e: e.reciprocal(
        g_rows[64:80, :], g_rows[64:80, :]), after=[tg])
    tg = kb.step("vector", lambda e: e.tensor_mul(
        g_rows[64:80, :], g_rows[64:80, :], er_rows[64:80, :]), after=[tg])

    for h in range(H):
        d_gbf = kb.dma("gpsimd", "gbf", lambda e, h=h: e.dma_start(
            out=ap3(g_bf[:], (h % 2) * row(g_bf) + S * (h // 2), [[row(g_bf), 1], [1, S]]),
            in_=g_rows[64 + h:65 + h, :]),
            after=[tg] if h == 0 else [])
    d_gbf = ("d", "gbf", H)

    gbp_g = nc.psum_tensor("ps_gbp", [P, 1024], F32)
    ps_gb = gbp_g.__enter__()
    norm_toks = []
    pg_free = [None, None]
    for c in range(NCH):
        for n0 in (0, 512, 1024, 1536):
            pb = (4 * c + n0 // 512) % 2
            mmg = kb.step("tensor", lambda e, c=c, n0=n0, pb=pb: e.matmul(
                ps_gb[:, 512 * pb:512 * (pb + 1)], ehalf[0:2, :],
                g_bf[0:2, S * c + n0:S * c + n0 + 512],
                start=True, stop=True),
                after=[d_gbf, ldc_all, pg_free[pb]] + (prev_group_bar if c == 0 and n0 == 0 else []))
            t_norm = kb.step("vector", lambda e, c=c, n0=n0, pb=pb: e.tensor_mul(
                ctxT[:, S * c + n0:S * c + n0 + 512],
                ctxT[:, S * c + n0:S * c + n0 + 512], ps_gb[:, 512 * pb:512 * (pb + 1)]),
                after=[mmg] + (ctx_write_toks[c] if n0 == 0 else []))
            pg_free[pb] = t_norm
            norm_toks.append(t_norm)

    if dbg == "D":
        toks = []
        for nm, src in (("ctxT", ctxT[:]), ("g", g_rows[64:80, :])):
            dst = {"ctxT": dbg_o["ctxT"], "g": dbg_o["g"][64:80, :]}[nm]
            toks.append(kb.dma("gpsimd", "dbg", lambda e, dst=dst, src=src: e.dma_start(
                out=dst, in_=src), after=norm_toks + [tg]))
        kb.step("sync", lambda e: e.wait_ge(kb.dma_sem("dbg"), 16 * len(toks)), inc=False)
        kb.emit()
        return nc

    dbar = kb.barrier()
    gbp_g.__exit__(None, None, None)
    rt_g.__exit__(None, None, None)
    for g in reversed(bd_guards):
        g.__exit__(None, None, None)

    efg_guards = []

    def EG(name, shape, dtype):
        g = nc.sbuf_tensor(name, shape, dtype)
        t = g.__enter__()
        efg_guards.append(g)
        return t

    SEG_W = S + 1
    h2T = EG("h2T", [P, NCH * SEG_W], BF16)
    xrT = EG("xrT", [P, NCH * 512], BF16)
    xkT = EG("xkT", [P, NCH * 512], BF16)
    r_tok = EG("r_tok", [P, 4 * D], BF16)
    kaT = EG("kaT", [P, NCH * 512], BF16)
    wsE = [EG(f"wsE{i}", [P, NCH * D], BF16) for i in range(3)]
    et1 = [t1_sb, t2_sb]
    et2 = [EG(f"et2_{i}", [P, D], F32) for i in range(2)]
    gf_sb = [t1_sb, t2_sb]
    gy_sb = et2
    gd_sb = EG("gd_sb", [P, 512], F32)
    bo_bc = EG("bo_bc", [P, D], BF16)
    assert nc.sbuf_bytes_remaining >= 0

    ps_tpE_g = nc.psum_tensor("ps_tpE", [P, 1024], BF16)
    ps_tpE = ps_tpE_g.__enter__()
    ps_c1_g = nc.psum_tensor("ps_c1", [P, 512], F32)
    ps_c1 = ps_c1_g.__enter__()
    ps_c2_g = nc.psum_tensor("ps_c2", [P, 512], F32)
    ps_c2 = ps_c2_g.__enter__()

    t_guard = kb.step("vector", lambda e: e.memset(
        ap3(h2T[:], 0, [[row(h2T), P], [SEG_W, NCH], [1, 1]]), 0.0), after=dbar)

    wtokE = {}
    for i_, nm in ((0, "wo"), (1, "wr"), (2, "wk2")):
        wtokE[nm] = kb.dma("gpsimd", f"ldw_{nm}", lambda e, i_=i_, nm=nm: e.dma_start(
            out=wsE[i_][:], in_=wf_h[nm]), after=dbar)
    wallE = kb.dma("gpsimd", "ldw_boE", lambda e: e.dma_start(
        out=bo_bc[:], in_=ap3(b_h["bo"], 0, [[0, 1], [0, P], [1, D]])), after=dbar)

    x_free = [None, None]
    h_free = [None, None]
    tp_free = t_guard
    h2T_evac = None
    wv2_dep = None
    et1_free = [None, None]
    et2_free = [None, None]
    for i in range(NT):
        sl = i % 2
        dst = ps_a if sl == 0 else ps_b
        free = psA_free if sl == 0 else psB_free
        mml = None
        for c in range(NCH):
            for nseg in range(2):
                mml = kb.step("tensor", lambda e, c=c, nseg=nseg, i=i, dst=dst: e.matmul(
                    dst[:, 512 * nseg:512 * (nseg + 1)],
                    ctxT[:, S * c + P * i:S * c + P * i + P],
                    wsE[0][:, D * c + 512 * nseg:D * c + 512 * (nseg + 1)],
                    start=(c == 0), stop=(c == NCH - 1)),
                    after=([norm_toks[4 * c + (P * i) // 512]] if nseg == 0 else [])
                    + ([wtokE["wo"], free] if (c == 0 and nseg == 0) else []))
        if i == NT - 1:
            wv2_dep = mml
        tao = kb.step("vector", lambda e, dst=dst, sl=sl: e.tensor_add(
            et1[sl][:], dst[:], bo_bc[:]),
            after=[mml, wallE, et1_free[sl]])
        if sl == 0:
            psA_free = tao
        else:
            psB_free = tao
        t_f = ln_stats(et1[sl][:], [tao])
        ty1 = ln_apply(et2[sl][:], et1[sl][:], t_f, [et2_free[sl]])
        et1_free[sl] = ty1
        tx = kb.dma("sync", f"ldx{sl}", lambda e, i=i, sl=sl: e.dma_start(
            out=x_tile[sl][:], in_=x_in[P * i:P * (i + 1), :]),
            after=x_free[sl] if isinstance(x_free[sl], list) else [x_free[sl]])
        tx1 = kb.step("vector", lambda e, sl=sl: e.tensor_add(
            x_tile[sl][:], x_tile[sl][:], et2[sl][:]), after=[ty1, tx])
        et2_free[sl] = tx1
        dx1 = kb.dma("sync", f"stx1{sl}", lambda e, i=i, sl=sl: e.dma_start(
            out=x1_scr[P * i:P * (i + 1), :], in_=x_tile[sl][:]), after=[tx1])
        t_f2 = ln_stats(x_tile[sl][:], [tx1])
        th2 = ln_apply(h_tile[sl][:], x_tile[sl][:], t_f2, [h_free[sl]])
        x_free[sl] = [dx1, th2]
        tps = None
        for c in range(NCH):
            tps = kb.step("tensor", lambda e, sl=sl, c=c: e.matmul(
                ps_tpE[:, P * c:P * (c + 1)], h_tile[sl][:, P * c:P * (c + 1)], eyeb[:],
                is_transpose=True, start=(c == 0), stop=(c == NCH - 1)),
                after=[th2, tp_free] if c == 0 else [])
        h_free[sl] = tps
        dst_h = ap3(h2T[:], 1 + P * i, [[row(h2T), P], [SEG_W, NCH], [1, P]])
        h2T_evac = kb.step("scalar", lambda e, dst_h=dst_h: e.copy(
            dst_h, ps_tpE[:].rearrange("p (c q) -> p c q", c=NCH)),
            after=[tps, h2T_evac])
        tp_free = h2T_evac

    wtokE["wv2"] = kb.dma("gpsimd", "ldw", lambda e: e.dma_start(
        out=wsE[0][:], in_=wf_h["wv2"]), after=[wv2_dep])

    c1_free = c2_free = None
    x_free = [None, None]
    n_out = 0
    mm1_last = None
    mm2_last = None
    mm3_last = None
    tf_prev = [None] * 4
    mix_free = None
    gf_free = list(et1_free)
    gy_free = list(et2_free)
    for sc in range(4):
        t0 = 512 * sc
        mixr_toks = []
        mixk_toks = []
        tdl = None
        tk_ = None
        for c in range(NCH):
            cur = ap3(h2T[:], 1 + t0 + SEG_W * c, [[row(h2T), P], [1, 512]])
            prev = ap3(h2T[:], t0 + SEG_W * c, [[row(h2T), P], [1, 512]])
            tdl = kb.step("vector", lambda e, cur=cur, prev=prev: e.tensor_sub(
                gd_sb[:], cur, prev),
                after=[h2T_evac, t_guard, mix_free, tk_])
            tr = kb.step("vector", lambda e, c=c, prev=prev: e.scalar_tensor_tensor(
                xrT[:, 512 * c:512 * (c + 1)], gd_sb[:],
                mur_sb[:, c:c + 1], prev, ALU.mult, ALU.add),
                after=[tdl, ldc_all, mm1_last])
            tk_ = kb.step("vector", lambda e, c=c, prev=prev: e.scalar_tensor_tensor(
                xkT[:, 512 * c:512 * (c + 1)], gd_sb[:],
                muk_sb[:, c:c + 1], prev, ALU.mult, ALU.add), after=[tr, mm2_last])
            mixr_toks.append(tr)
            mixk_toks.append(tk_)
        mixr = mixr_toks[-1]
        mixk = mixk_toks[-1]
        mix_free = mixk
        sig_toks = []
        for u in range(4):
            sl = u % 2
            dst = ps_a if sl == 0 else ps_b
            free = psA_free if sl == 0 else psB_free
            mml = None
            for c in range(NCH):
                for nseg in range(2):
                    mml = kb.step("tensor", lambda e, c=c, nseg=nseg, u=u, dst=dst: e.matmul(
                        dst[:, 512 * nseg:512 * (nseg + 1)],
                        xrT[:, 512 * c + P * u:512 * c + P * u + P],
                        wsE[1][:, D * c + 512 * nseg:D * c + 512 * (nseg + 1)],
                        start=(c == 0), stop=(c == NCH - 1)),
                        after=([mixr_toks[c]] if nseg == 0 else [])
                        + ([wtokE["wr"], free] if (c == 0 and nseg == 0) else []))
            ts = kb.step("scalar", lambda e, u=u, dst=dst: e.activation(
                out=r_tok[:, D * u:D * (u + 1)], in_=dst[:], func=AF.Sigmoid),
                after=[mml, tf_prev[u]])
            sig_toks.append(ts)
            if u == 3:
                mm1_last = mml
            if sl == 0:
                psA_free = ts
            else:
                psB_free = ts
        relu_toks = []
        sq_toks = []
        for dc in range(NCH):
            dst = ps_c1 if dc % 2 == 0 else ps_c2
            free = c1_free if dc % 2 == 0 else c2_free
            mml = None
            for c in range(NCH):
                mml = kb.step("tensor", lambda e, c=c, dc=dc, dst=dst: e.matmul(
                    dst[:, 0:512],
                    wsE[2][:, D * c + P * dc:D * c + P * dc + P],
                    xkT[:, 512 * c:512 * (c + 1)],
                    start=(c == 0), stop=(c == NCH - 1)),
                    after=[mixk_toks[c]] + ([wtokE["wk2"], free] if c == 0 else []))
            tr_ = kb.step("scalar", lambda e, dc=dc, dst=dst: e.activation(
                out=kaT[:, 512 * dc:512 * (dc + 1)], in_=dst[:], func=AF.Relu),
                after=[mml, mm3_last if dc == 0 else None])
            relu_toks.append(tr_)
            tsq_c = kb.step("scalar", lambda e, dc=dc: e.activation(
                out=kaT[:, 512 * dc:512 * (dc + 1)],
                in_=kaT[:, 512 * dc:512 * (dc + 1)], func=AF.Square),
                after=[tr_])
            sq_toks.append(tsq_c)
            if dc % 2 == 0:
                c1_free = tr_
            else:
                c2_free = tr_
            if dc == NCH - 1:
                mm2_last = mml
        for u in range(4):
            gt = 4 * sc + u
            sl = u % 2
            dst = ps_a if sl == 0 else ps_b
            free = psA_free if sl == 0 else psB_free
            mml = None
            for c in range(NCH):
                for nseg in range(2):
                    mml = kb.step("tensor", lambda e, c=c, nseg=nseg, u=u, dst=dst: e.matmul(
                        dst[:, 512 * nseg:512 * (nseg + 1)],
                        kaT[:, 512 * c + P * u:512 * c + P * u + P],
                        wsE[0][:, D * c + 512 * nseg:D * c + 512 * (nseg + 1)],
                        start=(c == 0), stop=(c == NCH - 1)),
                        after=([sq_toks[c]] if nseg == 0 else [])
                        + ([wtokE["wv2"], free] if (c == 0 and nseg == 0) else []))
            tf = kb.step("vector", lambda e, u=u, dst=dst, sl=sl: e.tensor_mul(
                gf_sb[sl][:], dst[:], r_tok[:, D * u:D * (u + 1)]),
                after=[mml, sig_toks[u], gf_free[sl]])
            tf_prev[u] = tf
            if u == 3:
                mm3_last = mml
            if sl == 0:
                psA_free = tf
            else:
                psB_free = tf
            t_f3 = ln_stats(gf_sb[sl][:], [tf])
            ty2 = ln_apply(gy_sb[sl][:], gf_sb[sl][:], t_f3, [gy_free[sl]])
            gf_free[sl] = ty2
            tx = kb.dma("sync", f"ldx1{sl}", lambda e, gt=gt, sl=sl: e.dma_start(
                out=x_tile[sl][:], in_=x1_scr[P * gt:P * (gt + 1), :]),
                after=[x_free[sl], ("d", "stx10", NT // 2), ("d", "stx11", NT // 2)])
            tout = kb.step("vector", lambda e, sl=sl: e.tensor_add(
                x_tile[sl][:], x_tile[sl][:], gy_sb[sl][:]), after=[ty2, tx])
            gy_free[sl] = tout
            dout = kb.dma("sync", f"st{sl}", lambda e, gt=gt, sl=sl: e.dma_start(
                out=out_h[P * gt:P * (gt + 1), :], in_=x_tile[sl][:]), after=[tout])
            x_free[sl] = dout
            n_out += 1

    kb.step("sync", lambda e: (e.wait_ge(kb.dma_sem("st0"), 16 * (n_out // 2)),
                               e.wait_ge(kb.dma_sem("st1"), 16 * (n_out // 2)))[-1],
            inc=False)

    kb.emit()
    ps_c2_g.__exit__(None, None, None)
    ps_c1_g.__exit__(None, None, None)
    ps_tpE_g.__exit__(None, None, None)
    for g in reversed(efg_guards):
        g.__exit__(None, None, None)
    return nc


_NC_CACHE = None


def kernel(**inputs):
    global _NC_CACHE
    chat, shat, chat8, shat8 = host_tables()
    x = np.ascontiguousarray(np.asarray(inputs["L0_attn_x"], dtype=np.float32))
    base = {
        "wo": wt_layout(inputs["L0_attn_Wo"]),
        "wr": wt_layout(inputs["L0_ffn_W_r"]),
        "wk2": wt_layout(inputs["L0_ffn_W_k"]),
        "wv2": wt_layout(inputs["L0_ffn_W_v"]),
        "bq": np.ascontiguousarray(inputs["L0_attn_bq"], dtype=np.float32),
        "bk": np.ascontiguousarray(inputs["L0_attn_bk"], dtype=np.float32),
        "bv": np.ascontiguousarray(inputs["L0_attn_bv"], dtype=np.float32),
        "bo": np.ascontiguousarray(inputs["L0_attn_bo"], dtype=np.float32),
        "mur": np.ascontiguousarray(
            np.asarray(inputs["L0_ffn_mu_r"], np.float32).reshape(NCH, P).T),
        "muk": np.ascontiguousarray(
            np.asarray(inputs["L0_ffn_mu_k"], np.float32).reshape(NCH, P).T),
        "ck": tile_major(chat), "sk": tile_major(shat),
        "eyeb": np.eye(P, dtype=ml_dtypes.bfloat16),
        "eyef": np.eye(P, dtype=np.float32),
        "mneg": np.where(np.arange(P)[None, :] <= np.arange(P)[:, None],
                         0.0, -1e30).astype(np.float32),
        "m01": np.where(np.arange(P)[:, None] <= np.arange(P)[None, :],
                        1.0, 0.0).astype(ml_dtypes.bfloat16),
        "ehalf": (np.arange(P)[None, :] // 64 == np.arange(2)[:, None]
                  ).astype(ml_dtypes.bfloat16),
    }
    for g in range(2):
        base[f"wq{g}"] = wt_half(inputs["L0_attn_Wq"], g)
        base[f"wk{g}"] = wt_half(inputs["L0_attn_Wk"], g)
        base[f"wv{g}"] = wt_half(inputs["L0_attn_Wv"], g)
    if _NC_CACHE is None:
        _NC_CACHE = build_nc()
    nc = _NC_CACHE
    in_maps = []
    for b in range(8):
        m = dict(base)
        m["x"] = np.ascontiguousarray(x[b])
        in_maps.append(m)
    res = run_bass_kernel_spmd(nc, in_maps, core_ids=list(range(8)))
    return np.stack([res.results[b]["out"] for b in range(8)], axis=0)
